# version 02 (final)
# baseline (speedup 1.0000x reference)
import os
import numpy as np
import ml_dtypes
import concourse.bass as bass
import concourse.mybir as mybir
from concourse.bass_utils import run_bass_kernel_spmd

F32 = mybir.dt.float32
BF16 = mybir.dt.bfloat16
AF = mybir.ActivationFunctionType
ALU = mybir.AluOpType
AX = mybir.AxisListType.X
BF = ml_dtypes.bfloat16

D = 1024
T = 8192
NT = T // 128
NC = D // 128
CH = 512
NCH = T // CH


class Sem:
    __slots__ = ("h", "n")

    def __init__(self, h):
        self.h = h
        self.n = 0


class Ctx:
    pass


def _e(cm):
    return cm.__enter__()


def inc(ins, s, k=1):
    ins.then_inc(s.h, k)
    s.n += k
    return ins


def bank_wait(c, e, i):
    for s, n in c.bfree[i]:
        if n > 0:
            e.wait_ge(s.h, n)


def bank_set(c, i, s):
    c.bfree[i] = [(s, s.n)]


def make_ctx(nc):
    c = Ctx()
    c.nc = nc
    c.dram = {}
    c.xT = [_e(nc.sbuf_tensor(f"xT{i}", [128, T], BF16)) for i in range(NC)]
    c.P = [_e(nc.psum_tensor(f"P{i}", [128, 512], F32)) for i in range(8)]
    c.bfree = {i: [] for i in range(8)}
    for name in ("gl", "sl", "s2", "pe", "ve", "sc", "st"):
        setattr(c, name, Sem(_e(nc.semaphore(name))))
    c.dsems = {}
    c.block = _e(nc.Block())
    return c


def dsem(c, name):
    if name not in c.dsems:
        c.dsems[name] = Sem(_e(c.nc.semaphore(name)))
    return c.dsems[name]


def din(c, name, shape, dtype=F32):
    if name not in c.dram:
        c.dram[name] = c.nc.dram_tensor(name, list(shape), dtype, kind="ExternalInput").ap()
    return c.dram[name]


def dscratch(c, name, shape, dtype):
    if name not in c.dram:
        c.dram[name] = c.nc.dram_tensor(name, list(shape), dtype, kind="Internal").ap()
    return c.dram[name]


def dout(c, name, shape, dtype=F32):
    if name not in c.dram:
        c.dram[name] = c.nc.dram_tensor(name, list(shape), dtype, kind="ExternalOutput").ap()
    return c.dram[name]



def const_inputs(c):
    din(c, "idb", (128, 128), BF16)
    din(c, "idf", (128, 128), F32)
    din(c, "onesall", (128, 128), F32)
    din(c, "shift64", (64, 128), BF16)
    din(c, "onescol_f", (128, 1), F32)
    din(c, "onescol_b", (128, 1), BF16)


def host_consts():
    idf = np.eye(128, dtype=np.float32)
    sh = np.zeros((64, 128), np.float32)
    for p in range(64):
        sh[p, p + 64] = 1.0
    return {
        "idb": idf.astype(BF),
        "idf": idf,
        "onesall": np.ones((128, 128), np.float32),
        "shift64": sh.astype(BF),
        "onescol_f": np.ones((128, 1), np.float32),
        "onescol_b": np.ones((128, 1), BF),
    }


def setup_const_sbuf(c):
    nc = c.nc
    c.idb = _e(nc.sbuf_tensor("idb_s", [128, 128], BF16))
    c.idf = _e(nc.sbuf_tensor("idf_s", [128, 128], F32))
    c.onesall = _e(nc.sbuf_tensor("onesall_s", [128, 128], F32))
    c.shift64 = _e(nc.sbuf_tensor("shift64_s", [64, 128], BF16))
    c.onescol_f = _e(nc.sbuf_tensor("onescol_f_s", [128, 1], F32))
    c.onescol_b = _e(nc.sbuf_tensor("onescol_b_s", [128, 1], BF16))
    sl, ve = c.sl, c.ve

    sco = dsem(c, "sconst")

    @c.block.sync
    def _(e):
        for nm, t in (("idb", c.idb), ("idf", c.idf), ("onesall", c.onesall),
                      ("shift64", c.shift64),
                      ("onescol_f", c.onescol_f), ("onescol_b", c.onescol_b)):
            inc(e.dma_start(out=t[:, :], in_=c.dram[nm][:, :]), sco, 16)
    c.sl_consts = sco.n

    c.cpool = _e(nc.sbuf_tensor("cpool", [128, 8], F32))
    cvals = [-1.0, 1e-5, -32.0, 1e-12, 0.5]

    @c.block.vector
    def _(e):
        m = None
        for j, v in enumerate(cvals):
            m = e.memset(c.cpool[:, j:j + 1], v)
        inc(m, ve, 1)
    for j, v in enumerate(cvals):
        nc.const_aps.register(F32, v, c.cpool[:, j:j + 1])
    c.ve_init = ve.n
    for eng in ("scalar", "tensor", "gpsimd", "sync"):
        getattr(c.block, eng)(lambda e: e.wait_ge(ve.h, c.ve_init))



def emit_ln_fm(c, rbuf, W, writer, ve_ready):
    pe, ve, sc = c.pe, c.ve, c.sc
    sq = c.lnsq
    srow = c.lnrow

    pe_sq = []
    for ot in range(NC):
        @c.block.scalar
        def _(e):
            e.wait_ge(ve.h, ve_ready[ot])
            if ot >= 2:
                e.wait_ge(pe.h, pe_sq[ot - 2])
            inc(e.square(sq[:, (ot % 2) * CH:(ot % 2) * CH + W], rbuf[ot][:, 0:W]), sc, 1)
        sc_sq = sc.n

        @c.block.tensor
        def _(e):
            if ot == 0:
                bank_wait(c, e, 5)
                bank_wait(c, e, 6)
            e.wait_ge(sc.h, sc_sq)
            e.matmul(c.P[5][0:1, 0:W], c.onescol_f[:, :], rbuf[ot][:, 0:W],
                     start=(ot == 0), stop=(ot == NC - 1))
            inc(e.matmul(c.P[6][0:1, 0:W], c.onescol_f[:, :],
                         sq[:, (ot % 2) * CH:(ot % 2) * CH + W],
                         start=(ot == 0), stop=(ot == NC - 1)), pe, 1)
        pe_sq.append(pe.n)
    pe_stats = pe.n

    @c.block.vector
    def _(e):
        e.wait_ge(pe.h, pe_stats)
        e.tensor_scalar_mul(srow[:, 0:W], c.P[5][0:1, 0:W], 1.0 / D)
        e.tensor_scalar_mul(srow[:, CH:CH + W], c.P[6][0:1, 0:W], 1.0 / D)
        e.tensor_mul(srow[:, 2 * CH:2 * CH + W], srow[:, 0:W], srow[:, 0:W])
        inc(e.tensor_sub(srow[:, CH:CH + W], srow[:, CH:CH + W], srow[:, 2 * CH:2 * CH + W]), ve, 1)
    bank_set(c, 5, ve)
    bank_set(c, 6, ve)
    ve_var = ve.n

    @c.block.scalar
    def _(e):
        e.wait_ge(ve.h, ve_var)
        inc(e.activation(srow[:, 2 * CH:2 * CH + W], srow[:, CH:CH + W], AF.Sqrt, bias=1e-5), sc, 1)
    sc_sd = sc.n

    @c.block.vector
    def _(e):
        e.wait_ge(sc.h, sc_sd)
        inc(e.reciprocal(srow[:, CH:CH + W], srow[:, 2 * CH:2 * CH + W]), ve, 1)
    ve_rstd = ve.n

    @c.block.tensor
    def _(e):
        bank_wait(c, e, 7)
        bank_wait(c, e, 4)
        e.wait_ge(ve.h, ve_rstd)
        inc(e.matmul(c.P[7][:, 0:W], c.onesall[0:1, :], srow[0:1, 0:W], start=True, stop=True), pe, 1)
        inc(e.matmul(c.P[4][:, 0:W], c.onesall[0:1, :], srow[0:1, CH:CH + W], start=True, stop=True), pe, 1)
    pe_exp = pe.n

    @c.block.vector
    def _(e):
        e.wait_ge(pe.h, pe_exp)
        for ot in range(NC):
            t = sq[:, (ot % 2) * CH:(ot % 2) * CH + W]
            e.tensor_sub(t, rbuf[ot][:, 0:W], c.P[7][:, 0:W])
            writer(ot, t, e)
    bank_set(c, 4, c.ve)
    bank_set(c, 7, c.ve)



def phase1_inputs(c):
    din(c, "xT", (D, T), F32)
    for nm in ("wqT", "wkT", "wvT", "woT"):
        din(c, nm, (D, D), F32)
    din(c, "bk_r", (1, D), F32)
    din(c, "bv_r", (1, D), F32)
    din(c, "bq_c", (128, NC), F32)
    din(c, "bo_c", (128, NC), F32)


def phase1_host(h, inp, b):
    h["xT"] = np.ascontiguousarray(inp["L0_attn_x"][b].T)
    h["wqT"] = np.ascontiguousarray(inp["L0_attn_Wq"].T)
    h["wkT"] = np.ascontiguousarray(inp["L0_attn_Wk"].T)
    h["wvT"] = np.ascontiguousarray(inp["L0_attn_Wv"].T)
    h["woT"] = np.ascontiguousarray(inp["L0_attn_Wo"].T)
    h["bk_r"] = inp["L0_attn_bk"].reshape(1, D).copy()
    h["bv_r"] = inp["L0_attn_bv"].reshape(1, D).copy()
    h["bq_c"] = np.ascontiguousarray(inp["L0_attn_bq"].reshape(NC, 128).T)
    h["bo_c"] = np.ascontiguousarray(inp["L0_attn_bo"].reshape(NC, 128).T)


def emit_phase1(c, save_x1=True):
    nc = c.nc
    P = c.P
    gl, sl, pe, ve, sc = c.gl, c.sl, c.pe, c.ve, c.sc

    phase_cms = []

    def alloc(name, shape, dtype):
        cm = nc.sbuf_tensor(name, shape, dtype)
        phase_cms.append(cm)
        return cm.__enter__()

    wk = [alloc(f"p1wk{i}", [128, D], BF16) for i in range(NC)]
    wv = [alloc(f"p1wv{i}", [128, D], BF16) for i in range(NC)]
    wq, wo = wk, wv
    bq_c = alloc("p1bqc", [128, NC], F32)
    bo_c = alloc("p1boc", [128, NC], F32)
    kvaug = alloc("p1kvaug", [128, 16 * 65], BF16)
    scratch_cms = []

    def salloc(name, shape, dtype):
        cm = nc.sbuf_tensor(name, shape, dtype)
        scratch_cms.append(cm)
        return cm.__enter__()

    c.lnsq = alloc("p1lnsq", [128, 2 * CH], F32)
    c.lnrow = alloc("p1lnrow", [1, 3 * CH], F32)
    bkE1 = salloc("p1bkE1", [128, D], F32)
    bvE = salloc("p1bvE", [128, D], F32)
    brow = salloc("p1brow", [1, D], F32)
    kb1 = salloc("p1kb1", [128, D], F32)
    ek = salloc("p1ek", [128, D], F32)
    Kf = salloc("p1Kf", [128, D], BF16)
    Vf = salloc("p1Vf", [128, D], BF16)
    if save_x1:
        dscratch(c, "x1s", (D, T), BF16)

    sxt = dsem(c, "sxt")
    swkv = dsem(c, "swkv")
    swqo = dsem(c, "swqo")
    sp1b = dsem(c, "sp1b")

    @c.block.gpsimd
    def _(e):
        for i in range(NC):
            inc(e.dma_start(out=c.xT[i][:, :], in_=c.dram["xT"][i * 128:(i + 1) * 128, :]), sxt, 16)
        for i in range(NC):
            inc(e.dma_start(out=wk[i][:, :], in_=c.dram["wkT"][i * 128:(i + 1) * 128, :]), swkv, 16)
            inc(e.dma_start(out=wv[i][:, :], in_=c.dram["wvT"][i * 128:(i + 1) * 128, :]), swkv, 16)
    gl_passA = (sxt.n, swkv.n)

    @c.block.sync
    def _(e):
        inc(e.dma_start(out=brow[:, :], in_=c.dram["bk_r"][:, :]), sp1b, 16)
        inc(e.dma_start(out=bq_c[:, :], in_=c.dram["bq_c"][:, :]), sp1b, 16)
        inc(e.dma_start(out=bo_c[:, :], in_=c.dram["bo_c"][:, :]), sp1b, 16)
    sl_bk = sp1b.n

    @c.block.tensor
    def _(e):
        e.wait_ge(sp1b.h, sl_bk)
        e.wait_ge(dsem(c, 'sconst').h, c.sl_consts)
        e.matmul(P[6][:, 0:512], c.onesall[0:1, :], brow[0:1, 0:512], start=True, stop=True)
        inc(e.matmul(P[7][:, 0:512], c.onesall[0:1, :], brow[0:1, 512:1024], start=True, stop=True), pe, 1)
    pe_bk = pe.n

    @c.block.vector
    def _(e):
        e.wait_ge(pe.h, pe_bk)
        e.tensor_scalar_add(bkE1[:, 0:512], P[6][:, 0:512], 1.0)
        inc(e.tensor_scalar_add(bkE1[:, 512:1024], P[7][:, 0:512], 1.0), ve, 1)
    ve_bk = ve.n

    @c.block.sync
    def _(e):
        e.wait_ge(sp1b.h, sp1b.n)
        e.wait_ge(ve.h, ve_bk)
        inc(e.dma_start(out=brow[:, :], in_=c.dram["bv_r"][:, :]), sp1b, 16)
    sl_bv = sp1b.n

    @c.block.tensor
    def _(e):
        e.wait_ge(sp1b.h, sl_bv)
        e.wait_ge(ve.h, ve_bk)
        e.matmul(P[6][:, 0:512], c.onesall[0:1, :], brow[0:1, 0:512], start=True, stop=True)
        inc(e.matmul(P[7][:, 0:512], c.onesall[0:1, :], brow[0:1, 512:1024], start=True, stop=True), pe, 1)
    pe_bv = pe.n

    @c.block.vector
    def _(e):
        e.wait_ge(pe.h, pe_bv)
        e.tensor_copy(bvE[:, 0:512], P[6][:, 0:512])
        inc(e.tensor_copy(bvE[:, 512:1024], P[7][:, 0:512]), ve, 1)

    pe0, ve0, sc0 = pe.n, ve.n, sc.n

    @c.block.tensor
    def _(e):
        e.wait_ge(sxt.h, gl_passA[0])
        e.wait_ge(swkv.h, gl_passA[1])
        e.wait_ge(ve.h, ve0)
        for tt in range(NT):
            ts = slice(tt * 128, tt * 128 + 128)
            if tt > 0:
                e.wait_ge(ve.h, ve0 + 6 * (tt - 1) + 4)
            for d in range(2):
                m = None
                for ci in range(NC):
                    m = e.matmul(P[d][:, :], c.xT[ci][:, ts], wk[ci][:, d * 512:(d + 1) * 512],
                                 start=(ci == 0), stop=(ci == NC - 1))
                inc(m, pe, 1)
            for d in range(2):
                m = None
                for ci in range(NC):
                    m = e.matmul(P[2 + d][:, :], c.xT[ci][:, ts], wv[ci][:, d * 512:(d + 1) * 512],
                                 start=(ci == 0), stop=(ci == NC - 1))
                inc(m, pe, 1)
            e.wait_ge(ve.h, ve0 + 6 * tt + 6)
            m = None
            for h in range(16):
                bank = P[4] if h < 8 else P[5]
                col = (h % 8) * 64
                m = e.matmul(bank[0:64, col:col + 64], Kf[:, h * 64:h * 64 + 64],
                             Vf[:, h * 64:h * 64 + 64], start=(tt == 0), stop=(tt == NT - 1))
            for h in range(16):
                m = e.matmul(P[6][0:64, h:h + 1], Kf[:, h * 64:h * 64 + 64],
                             c.onescol_b[:, :], start=(tt == 0), stop=(tt == NT - 1))
            inc(m, pe, 1)

    @c.block.scalar
    def _(e):
        for tt in range(NT):
            for d in range(2):
                e.wait_ge(ve.h, ve0 + 6 * tt + 1 + d)
                inc(e.activation(ek[:, d * 512:(d + 1) * 512], kb1[:, d * 512:(d + 1) * 512],
                                 AF.Exp, bias=-1.0), sc, 1)

    @c.block.vector
    def _(e):
        for tt in range(NT):
            if tt > 0:
                e.wait_ge(sc.h, sc0 + 2 * (tt - 1) + 2)
                e.wait_ge(pe.h, pe0 + 5 * (tt - 1) + 5)
            for d in range(2):
                e.wait_ge(pe.h, pe0 + 5 * tt + 1 + d)
                inc(e.tensor_add(kb1[:, d * 512:(d + 1) * 512], P[d][:, :],
                                 bkE1[:, d * 512:(d + 1) * 512]), ve, 1)
            for d in range(2):
                e.wait_ge(pe.h, pe0 + 5 * tt + 3 + d)
                inc(e.tensor_add(Vf[:, d * 512:(d + 1) * 512], P[2 + d][:, :],
                                 bvE[:, d * 512:(d + 1) * 512]), ve, 1)
            for d in range(2):
                e.wait_ge(sc.h, sc0 + 2 * tt + 1 + d)
                e.tensor_scalar_min(ek[:, d * 512:(d + 1) * 512], ek[:, d * 512:(d + 1) * 512], 1.0)
                inc(e.tensor_max(Kf[:, d * 512:(d + 1) * 512], kb1[:, d * 512:(d + 1) * 512],
                                 ek[:, d * 512:(d + 1) * 512]), ve, 1)

    pe_passA = pe.n

    @c.block.vector
    def _(e):
        e.wait_ge(pe.h, pe_passA)
        m = None
        for h in range(16):
            bank = P[4] if h < 8 else P[5]
            col = (h % 8) * 64
            e.tensor_copy(kvaug[0:64, h * 65:h * 65 + 64], bank[0:64, col:col + 64])
            m = e.tensor_copy(kvaug[0:64, h * 65 + 64:h * 65 + 65],
                              P[6][0:64, h:h + 1])
        inc(m, ve, 1)
    for i in range(8):
        bank_set(c, i, ve)
    ve_kvevac = ve.n

    smisc = dsem(c, "smisc")

    @c.block.sync
    def _(e):
        e.wait_ge(ve.h, ve_kvevac)
        inc(e.dma_start(out=kvaug[64:128, :], in_=kvaug[0:64, :]), smisc, 16)
    sl_kvdup = smisc.n

    @c.block.gpsimd
    def _(e):
        e.wait_ge(pe.h, pe_passA)
        for i in range(NC):
            inc(e.dma_start(out=wq[i][:, :], in_=c.dram["wqT"][i * 128:(i + 1) * 128, :]), swqo, 16)
            inc(e.dma_start(out=wo[i][:, :], in_=c.dram["woT"][i * 128:(i + 1) * 128, :]), swqo, 16)
    gl_passB = swqo.n

    for cm in reversed(scratch_cms):
        cm.__exit__(None, None, None)
    qb1 = alloc("p1qb1", [128, CH], F32)
    eq = alloc("p1eq", [128, CH], F32)
    qf = [alloc(f"p1qf{i}", [128, CH], BF16) for i in range(NC)]
    zrow = alloc("p1zrow", [128, CH], F32)
    zs = alloc("p1zs", [64, CH], F32)
    o64 = alloc("p1o64", [64, CH], BF16)
    oT = qf
    rbuf = [alloc(f"p1rb{i}", [128, CH], F32) for i in range(NC)]

    qb1_free = [None, None]
    qf_free = [None] * NC

    @c.block.tensor
    def _(e):
        e.wait_ge(swqo.h, gl_passB)
        e.wait_ge(dsem(c, "smisc").h, sl_kvdup)

    for chn in range(NCH):
        cs = slice(chn * CH, chn * CH + CH)
        for qt in range(NC):
            bi = qt % 2

            @c.block.tensor
            def _(e):
                bank_wait(c, e, bi)
                m = None
                for ci in range(NC):
                    m = e.matmul(P[bi][:, :], wq[ci][:, qt * 128:(qt + 1) * 128],
                                 c.xT[ci][:, cs], start=(ci == 0), stop=(ci == NC - 1))
                inc(m, pe, 1)
            pe_q = pe.n

            @c.block.vector
            def _(e):
                e.wait_ge(pe.h, pe_q)
                if qb1_free[0] is not None:
                    s, n = qb1_free[0]
                    e.wait_ge(s.h, n)
                inc(e.tensor_scalar(qb1[:, :], P[bi][:, :],
                                    bq_c[:, qt:qt + 1], 1.0, ALU.add, ALU.add), ve, 1)
            bank_set(c, bi, ve)
            ve_qb = ve.n

            @c.block.scalar
            def _(e):
                e.wait_ge(ve.h, ve_qb)
                inc(e.activation(eq[:, :], qb1[:, :], AF.Exp, bias=-1.0), sc, 1)
            sc_eq = sc.n

            @c.block.vector
            def _(e):
                e.wait_ge(sc.h, sc_eq)
                if qf_free[qt] is not None:
                    s, n = qf_free[qt]
                    e.wait_ge(s.h, n)
                e.tensor_scalar_min(eq[:, :], eq[:, :], 1.0)
                inc(e.tensor_max(qf[qt][:, :], qb1[:, :], eq[:, :]), ve, 1)
            qb1_free[0] = (sc, sc.n)

        for h in range(16):
            bi = 2 + (h % 2)
            qtile = h // 2
            off = 64 * (h % 2)

            @c.block.tensor
            def _(e):
                bank_wait(c, e, bi)
                e.wait_ge(ve.h, ve.n)
                inc(e.matmul(P[bi][0:65, :], kvaug[off:off + 64, h * 65:h * 65 + 65],
                             qf[qtile][off:off + 64, :], start=True, stop=True), pe, 1)
            pe_aug = pe.n

            @c.block.vector
            def _(e):
                e.wait_ge(pe.h, pe_aug)
                inc(e.tensor_scalar_add(zrow[64:65, :], P[bi][64:65, :], 1e-6), ve, 1)
                e.wait_ge(ve.h, ve.n)
                inc(e.reciprocal(zrow[64:65, :], zrow[64:65, :]), ve, 1)
            ve_zr = ve.n

            @c.block.tensor
            def _(e):
                bank_wait(c, e, 4)
                e.wait_ge(ve.h, ve_zr)
                inc(e.matmul(P[4][0:64, :], c.onesall[64:65, 0:64], zrow[64:65, :],
                             start=True, stop=True), pe, 1)
            pe_zexp = pe.n

            @c.block.vector
            def _(e):
                e.wait_ge(pe.h, pe_zexp)
                inc(e.tensor_copy(zs[0:64, :], P[4][0:64, :]), ve, 1)
                dst = oT[qtile][0:64, :] if off == 0 else o64[0:64, :]
                inc(e.scalar_tensor_tensor(dst, P[bi][0:64, :],
                                           1.0, zs[0:64, :], ALU.mult, ALU.mult), ve, 1)
            bank_set(c, 4, ve)
            bank_set(c, bi, ve)
            if off:
                ve_o64 = ve.n

                @c.block.tensor
                def _(e):
                    bank_wait(c, e, 4)
                    e.wait_ge(ve.h, ve_o64)
                    inc(e.matmul(P[4][:, :], c.shift64[:, :], o64[0:64, :],
                                 start=True, stop=True), pe, 1)
                pe_sh = pe.n

                @c.block.vector
                def _(e):
                    e.wait_ge(pe.h, pe_sh)
                    inc(e.tensor_copy(oT[qtile][64:128, :], P[4][64:128, :]), ve, 1)
                bank_set(c, 4, ve)
        for qt in range(NC):
            qf_free[qt] = (pe, pe.n)

        ve_ready = []
        for ot in range(NC):
            bi = ot % 2

            @c.block.tensor
            def _(e):
                bank_wait(c, e, bi)
                e.wait_ge(ve.h, ve.n)
                m = None
                for ci in range(NC):
                    m = e.matmul(P[bi][:, :], wo[ci][:, ot * 128:(ot + 1) * 128],
                                 oT[ci][:, :], start=(ci == 0), stop=(ci == NC - 1))
                inc(m, pe, 1)
            pe_o = pe.n

            @c.block.scalar
            def _(e):
                e.wait_ge(pe.h, pe_o)
                inc(e.add(P[bi][:, :], P[bi][:, :], bo_c[:, ot:ot + 1]), sc, 1)
            sc_bo = sc.n

            @c.block.vector
            def _(e):
                e.wait_ge(sc.h, sc_bo)
                inc(e.tensor_add(rbuf[ot][:, :], P[bi][:, :], c.xT[ot][:, cs]), ve, 1)
            bank_set(c, bi, ve)
            ve_ready.append(ve.n)

        def writer(ot, tmp, e, cs=cs):
            inc(e.tensor_mul(c.xT[ot][:, cs], tmp, c.P[4][:, 0:CH]), ve, 1)

        emit_ln_fm(c, rbuf, CH, writer, ve_ready)
        ve_ln = ve.n

        if save_x1:
            @c.block.sync
            def _(e):
                e.wait_ge(ve.h, ve_ln)
                m = None
                for ot in range(NC):
                    m = inc(e.dma_start(out=c.dram["x1s"][ot * 128:(ot + 1) * 128, cs],
                                        in_=c.xT[ot][:, cs]), dsem(c, "sx1w"), 16)

    for cm in reversed(phase_cms):
        cm.__exit__(None, None, None)




def emit_ln2(c, srcs, W, writer, ready):
    pe, ve, sc = c.pe, c.ve, c.sc
    sq = c.lnsq
    srow = c.lnrow
    pe_sq = []
    for ot in range(NC):
        @c.block.scalar
        def _(e):
            if ready[ot] is not None:
                s, n = ready[ot]
                e.wait_ge(s.h, n)
            if ot >= 2:
                e.wait_ge(pe.h, pe_sq[ot - 2])
            inc(e.square(sq[:, (ot % 2) * CH:(ot % 2) * CH + W], srcs[ot][:, 0:W]), sc, 1)
        sc_sq = sc.n

        @c.block.tensor
        def _(e):
            if ot == 0:
                bank_wait(c, e, 5)
                bank_wait(c, e, 6)
            if ready[ot] is not None:
                s, n = ready[ot]
                e.wait_ge(s.h, n)
            e.wait_ge(sc.h, sc_sq)
            ones = c.onescol_b if srcs[ot].dtype == BF16 else c.onescol_f
            e.matmul(c.P[5][0:1, 0:W], ones[:, :], srcs[ot][:, 0:W],
                     start=(ot == 0), stop=(ot == NC - 1))
            inc(e.matmul(c.P[6][0:1, 0:W], c.onescol_f[:, :],
                         sq[:, (ot % 2) * CH:(ot % 2) * CH + W],
                         start=(ot == 0), stop=(ot == NC - 1)), pe, 1)
        pe_sq.append(pe.n)
    pe_stats = pe.n

    @c.block.vector
    def _(e):
        e.wait_ge(pe.h, pe_stats)
        e.tensor_scalar_mul(srow[:, 0:W], c.P[5][0:1, 0:W], 1.0 / D)
        e.tensor_scalar_mul(srow[:, CH:CH + W], c.P[6][0:1, 0:W], 1.0 / D)
        e.tensor_mul(c.lnsq[0:1, 0:W], srow[:, 0:W], srow[:, 0:W])
        inc(e.tensor_sub(srow[:, CH:CH + W], srow[:, CH:CH + W], c.lnsq[0:1, 0:W]), ve, 1)
    bank_set(c, 5, ve)
    bank_set(c, 6, ve)
    ve_var = ve.n

    @c.block.scalar
    def _(e):
        e.wait_ge(ve.h, ve_var)
        inc(e.activation(c.lnsq[0:1, 0:W], srow[:, CH:CH + W], AF.Sqrt, bias=1e-5), sc, 1)
    sc_sd = sc.n

    @c.block.vector
    def _(e):
        e.wait_ge(sc.h, sc_sd)
        inc(e.reciprocal(srow[:, CH:CH + W], c.lnsq[0:1, 0:W]), ve, 1)
    ve_rstd = ve.n

    @c.block.tensor
    def _(e):
        bank_wait(c, e, 7)
        bank_wait(c, e, 4)
        e.wait_ge(ve.h, ve_rstd)
        inc(e.matmul(c.P[7][:, 0:W], c.onesall[0:1, :], srow[0:1, 0:W], start=True, stop=True), pe, 1)
        inc(e.matmul(c.P[4][:, 0:W], c.onesall[0:1, :], srow[0:1, CH:CH + W], start=True, stop=True), pe, 1)
    pe_exp = pe.n

    @c.block.vector
    def _(e):
        e.wait_ge(pe.h, pe_exp)
        for ot in range(NC):
            t = sq[:, (ot % 2) * CH:(ot % 2) * CH + W]
            e.tensor_sub(t, srcs[ot][:, 0:W], c.P[7][:, 0:W])
            writer(ot, t, e)
    bank_set(c, 4, c.ve)
    bank_set(c, 7, c.ve)



HID = 2730
NFK = 22
FKS = [128] * 21 + [42]


def phase2_inputs(c):
    for l in range(2):
        din(c, f"qkvw{l}", (D, 1536), F32)
        din(c, f"fwo{l}", (512, D), F32)
        din(c, f"w1a_{l}", (D, 2816), F32)
        din(c, f"w1g_{l}", (D, 2816), F32)
        din(c, f"w2_{l}", (HID, D), F32)
        dscratch(c, f"w1ab{l}", (D, 2816), BF16)
        dscratch(c, f"w1gb{l}", (D, 2816), BF16)
        dscratch(c, f"w2b{l}", (HID, D), BF16)
        dscratch(c, f"qkvb{l}", (D, 1536), BF16)
        dscratch(c, f"fwob{l}", (512, D), BF16)
        din(c, f"qs{l}", (128, 1), F32)
        din(c, f"ks{l}", (128, 1), F32)
        din(c, f"lng{l}", (128, NC), F32)
        din(c, f"lnb{l}", (128, NC), F32)
        din(c, f"b1a{l}", (128, 24), F32)
        din(c, f"b1g{l}", (128, 24), F32)
        din(c, f"b2c{l}", (128, NC), F32)
    for nm in ("c1q", "c2q", "c1k", "c2k"):
        din(c, nm, (128, T), F32)
    din(c, "lmaskb", (512, 1024), BF16)
    din(c, "blk01", (128, 2), F32)
    din(c, "expand2", (2, 128), F32)
    din(c, "S64", (128, 128), F32)


def phase2_host(h, inp):
    DH, WW = 64, 512
    for l in range(2):
        h[f"qkvw{l}"] = np.ascontiguousarray(inp["L0_ffn_wqkv"][l])
        h[f"fwo{l}"] = np.ascontiguousarray(inp["L0_ffn_wo"][l])
        w1p = np.zeros((D, 2816), np.float32)
        w1p[:, 0:HID] = inp["L0_ffn_w1"][l][:, 0:HID]
        h[f"w1a_{l}"] = w1p
        w1p = np.zeros((D, 2816), np.float32)
        w1p[:, 0:HID] = inp["L0_ffn_w1"][l][:, HID:2 * HID]
        h[f"w1g_{l}"] = w1p
        h[f"w2_{l}"] = np.ascontiguousarray(inp["L0_ffn_w2"][l])
        h[f"qs{l}"] = np.tile(inp["L0_ffn_q_scale"][l], 2).reshape(128, 1).copy()
        h[f"ks{l}"] = np.tile(inp["L0_ffn_k_scale"][l], 2).reshape(128, 1).copy()
        h[f"lng{l}"] = np.ascontiguousarray(inp["L0_ffn_ln_g"][l].reshape(NC, 128).T)
        h[f"lnb{l}"] = np.ascontiguousarray(inp["L0_ffn_ln_b"][l].reshape(NC, 128).T)
        b1 = inp["L0_ffn_b1"][l]
        ba = np.zeros((128, 24), np.float32)
        bg = np.zeros((128, 24), np.float32)
        for k in range(NFK):
            ba[0:FKS[k], k] = b1[k * 128:k * 128 + FKS[k]]
            bg[0:FKS[k], k] = b1[HID + k * 128:HID + k * 128 + FKS[k]]
        h[f"b1a{l}"] = ba
        h[f"b1g{l}"] = bg
        h[f"b2c{l}"] = np.ascontiguousarray(inp["L0_ffn_b2"][l].reshape(NC, 128).T)
    half = np.arange(0, DH, 2, dtype=np.float64)
    invf = 1.0 / 10000.0 ** (half / DH)
    t = np.arange(T, dtype=np.float64)
    fr = t[:, None] * invf[None, :]
    base = (half + 0.4 * DH) / (1.4 * DH)
    power = (t - T // 2) / (WW // 2)
    sc = base[None, :] ** power[:, None]
    fr2 = np.concatenate([fr, fr], 1)
    sc2 = np.concatenate([sc, sc], 1)
    c1q = (8.0 * np.cos(fr2) * sc2).T
    c2q = (8.0 * np.sin(fr2) * sc2).T
    c1k = (np.cos(fr2) / sc2).T
    c2k = (np.sin(fr2) / sc2).T
    for nm, v in (("c1q", c1q), ("c2q", c2q), ("c1k", c1k), ("c2k", c2k)):
        h[nm] = np.ascontiguousarray(np.concatenate([v, v], 0).astype(np.float32))
    i_rel = np.arange(512)
    jj = np.arange(1024)
    keep = (jj[None, :] >= i_rel[:, None]) & (jj[None, :] <= i_rel[:, None] + 512)
    h["lmaskb"] = np.where(keep, np.float32(0.0), np.float32(-1e30)).astype(BF)
    blk = np.zeros((128, 2), np.float32)
    blk[0:64, 0] = 1.0
    blk[64:128, 1] = 1.0
    h["blk01"] = blk
    ex2 = np.zeros((2, 128), np.float32)
    ex2[0, 0:64] = 1.0
    ex2[1, 64:128] = 1.0
    h["expand2"] = ex2
    S = np.zeros((128, 128), np.float32)
    for blk0 in (0, 64):
        for m in range(32):
            S[blk0 + m + 32, blk0 + m] = -1.0
            S[blk0 + m, blk0 + m + 32] = 1.0
    h["S64"] = S


def emit_l0ffn_layer(c, l, alloc, masks, tabs_bufs, dbgw=None):
    nc, P = c.nc, c.P
    gl, sl, s2, pe, ve, sc = c.gl, c.sl, c.s2, c.pe, c.ve, c.sc
    (qkvstr, fwostr, shz, qq, kr, vt, vtok, attnT, oTw, o64b,
     tabs, pscr, sq1, nrow, rn2, simf, attn_s, acol, stash) = tabs_bufs

    pe_prev, sc_prev, ve_prev = pe.n, sc.n, ve.n

    @c.block.vector
    def _(e):
        if l > 0:
            e.wait_ge(pe.h, pe_prev)
            e.wait_ge(sc.h, sc_prev)
        inc(e.memset(stash[:, :], 0.0), ve, 1)
    ve_stash0 = ve.n

    sco = dsem(c, "sconst")

    @c.block.sync
    def _(e):
        e.wait_ge(dsem(c, 'sconv1').h, c.gl_qkvconv)
        if l > 0:
            e.wait_ge(pe.h, pe_prev)
            e.wait_ge(sc.h, sc_prev)
            e.wait_ge(ve.h, ve_prev)
        e.wait_ge(sco.h, sco.n)
        inc(e.dma_start(out=masks[:, :].rearrange("p (a f) -> p a f", a=4),
                        in_=c.dram["lmaskb"][:, :].rearrange("(a p) f -> p a f", p=128)), sco, 16)
    c.sl_masks = sco.n

    veA_win = []
    c.sc_exp_last = None
    c.pe_tp_last = None
    c.qkvstr_free = [None, None]
    c.fwostr_free = [None, None]
    qq_free = None
    shz_free = None
    kr_free = [None, None]
    vtok_free = [None, None]
    vt_free = None
    tabs_free = None
    attnT_free = None
    for w in range(NCH):
        t0 = w * CH
        slot = w % 2
        pslot = (w - 1) % 2
        @c.block.sync
        def _(e):
            if tabs_free is not None:
                s, n = tabs_free
                e.wait_ge(s.h, n)
            if attnT_free is not None:
                s, n = attnT_free
                e.wait_ge(s.h, n)
            inc(e.dma_start(out=tabs_bufs[10][:, 0:512], in_=c.dram["c1q"][:, t0:t0 + CH]), dsem(c, "stab"), 16)
            inc(e.dma_start(out=tabs_bufs[10][:, 512:1024], in_=c.dram["c2q"][:, t0:t0 + CH]), dsem(c, "stab"), 16)
        sl_tabq = dsem(c, "stab").n
        tabs_t = tabs_bufs[10]

        @c.block.vector
        def _(e):
            if w == 0:
                e.wait_ge(ve.h, ve_stash0)
            if shz_free is not None:
                s, n = shz_free
                e.wait_ge(s.h, n)
            m = None
            for j in range(4):
                m = e.tensor_copy(shz[:, j * CH:j * CH + 1], stash[:, 4 + j:5 + j])
                m = e.tensor_copy(shz[:, j * CH + 1:j * CH + CH],
                                  c.xT[4 + j][:, t0:t0 + CH - 1])
            inc(m, ve, 1)
        ve_shz = ve.n

        for ot in range(12):
            b = ot % 3
            wslot = ot % 2

            @c.block.sync
            def _(e):
                if l > 0 and w == 0 and ot < 2:
                    e.wait_ge(pe.h, pe_prev)
                    e.wait_ge(sc.h, sc_prev)
                    e.wait_ge(ve.h, ve_prev)
                if c.qkvstr_free[wslot] is not None:
                    s, n = c.qkvstr_free[wslot]
                    e.wait_ge(s.h, n)
                inc(e.dma_start(out=qkvstr[:, wslot * 1024:(wslot + 1) * 1024]
                                .rearrange("p (a f) -> p a f", a=8),
                                in_=c.dram[f"qkvb{l}"][:, ot * 128:(ot + 1) * 128]
                                .rearrange("(a p) f -> p a f", p=128)), dsem(c, f"sqkv{wslot}"), 16)
            sl_qw = dsem(c, f"sqkv{wslot}").n

            @c.block.tensor
            def _(e):
                bank_wait(c, e, b)
                e.wait_ge(dsem(c, f"sqkv{wslot}").h, sl_qw)
                if ot == 0:
                    e.wait_ge(ve.h, ve_shz)
                m = None
                for ci in range(NC):
                    rhs = c.xT[ci][:, t0:t0 + CH] if ci < 4 else shz[:, (ci - 4) * CH:(ci - 3) * CH]
                    m = e.matmul(P[b][:, :], qkvstr[:, wslot * 1024 + ci * 128:wslot * 1024 + (ci + 1) * 128],
                                 rhs, start=(ci == 0), stop=(ci == NC - 1))
                inc(m, pe, 1)
            pe_proj = pe.n
            c.qkvstr_free[wslot] = (pe, pe.n)
            if ot == 11:
                shz_free = (pe, pe.n)

            if ot < 8:
                kind = "q" if ot < 4 else "k"
                if kind == "k" and ot == 4:
                    @c.block.sync
                    def _(e):
                        e.wait_ge(ve.h, ve.n)
                        inc(e.dma_start(out=tabs_t[:, 0:512], in_=c.dram["c1k"][:, t0:t0 + CH]), dsem(c, "stab"), 16)
                        inc(e.dma_start(out=tabs_t[:, 512:1024], in_=c.dram["c2k"][:, t0:t0 + CH]), dsem(c, "stab"), 16)
                sl_tab = dsem(c, "stab").n
                scol = c.dramcols[f"qs{l}"] if kind == "q" else c.dramcols[f"ks{l}"]

                @c.block.scalar
                def _(e):
                    e.wait_ge(pe.h, pe_proj)
                    if c.pe_tp_last is not None:
                        e.wait_ge(pe.h, c.pe_tp_last)
                    inc(e.square(sq1[:, 0:512], P[b][:, :]), sc, 1)
                sc_sq = sc.n

                @c.block.tensor
                def _(e):
                    bank_wait(c, e, 3)
                    e.wait_ge(sc.h, sc_sq)
                    inc(e.matmul(P[3][0:2, :], c.blk01[:, :], sq1[:, 0:512],
                                 start=True, stop=True), pe, 1)
                pe_ss = pe.n

                @c.block.scalar
                def _(e):
                    e.wait_ge(pe.h, pe_ss)
                    inc(e.activation(nrow[:, :], P[3][0:2, :], AF.Sqrt), sc, 1)
                bank_set(c, 3, sc)
                sc_n = sc.n

                @c.block.vector
                def _(e):
                    e.wait_ge(sc.h, sc_n)
                    if c.pe_tp_last is not None:
                        e.wait_ge(pe.h, c.pe_tp_last)
                    inc(e.tensor_scalar_max(nrow[:, :], nrow[:, :], 1e-12), ve, 1)
                    e.wait_ge(ve.h, ve.n)
                    inc(e.reciprocal(rn2[:, :], nrow[:, :]), ve, 1)
                    inc(e.tensor_scalar_mul(sq1[:, 512:1024], P[b][:, :], scol), ve, 1)
                ve_rn = ve.n
                c.bfree[b] = [(sc, sc_sq), (ve, ve_rn)]

                @c.block.tensor
                def _(e):
                    bank_wait(c, e, 4)
                    bank_wait(c, e, 5)
                    e.wait_ge(ve.h, ve_rn)
                    inc(e.matmul(P[4][:, :], c.expand2[:, :], rn2[:, :], start=True, stop=True), pe, 1)
                    inc(e.matmul(P[5][:, :], c.S64[:, :], sq1[:, 512:1024], start=True, stop=True), pe, 1)
                pe_rot = pe.n

                if kind == "q":
                    dst = qq[:, ot * CH:(ot + 1) * CH]
                    dfree = qq_free
                else:
                    dst = kr[:, (slot * 4 + ot - 4) * CH:(slot * 4 + ot - 3) * CH]
                    dfree = kr_free[slot]

                @c.block.vector
                def _(e):
                    e.wait_ge(pe.h, pe_rot)
                    e.wait_ge(dsem(c, 'stab').h, sl_tab)
                    if c.sc_exp_last is not None:
                        e.wait_ge(sc.h, c.sc_exp_last)
                    if dfree is not None:
                        s, n = dfree
                        e.wait_ge(s.h, n)
                    e.tensor_mul(pscr[:, 0:512], sq1[:, 512:1024], tabs_t[:, 0:512])
                    e.scalar_tensor_tensor(pscr[:, 512:1024], P[5][:, :], 1.0,
                                           tabs_t[:, 512:1024], ALU.mult, ALU.mult)
                    e.tensor_add(pscr[:, 0:512], pscr[:, 0:512], pscr[:, 512:1024])
                    inc(e.tensor_mul(dst, pscr[:, 0:512], P[4][:, :]), ve, 1)
                bank_set(c, 4, ve)
                bank_set(c, 5, ve)
                tabs_free = (ve, ve.n)
            else:
                @c.block.vector
                def _(e):
                    e.wait_ge(pe.h, pe_proj)
                    if ot == 8 and vt_free is not None:
                        s, n = vt_free
                        e.wait_ge(s.h, n)
                    inc(e.tensor_copy(vt[:, (ot - 8) * CH:(ot - 7) * CH], P[b][:, :]), ve, 1)
                bank_set(c, b, ve)
        ve_vt = ve.n

        for rnd in range(2):
            @c.block.tensor
            def _(e):
                bank_wait(c, e, 5)
                if rnd == 0:
                    e.wait_ge(ve.h, ve_vt)
                    if vtok_free[slot] is not None:
                        s, n = vtok_free[slot]
                        e.wait_ge(s.h, n)
                m = None
                for j in range(8):
                    p_, jtl = divmod(rnd * 8 + j, 4)
                    m = e.transpose(P[5][:, :].bitcast(BF16)[:, j * 128:(j + 1) * 128],
                                    vt[:, p_ * CH + jtl * 128:p_ * CH + (jtl + 1) * 128],
                                    c.idb[:, :])
                inc(m, pe, 1)
            pe_vt = pe.n

            @c.block.vector
            def _(e):
                e.wait_ge(pe.h, pe_vt)
                inc(e.tensor_copy(vtok[:, slot * 2048 + rnd * 1024:slot * 2048 + (rnd + 1) * 1024],
                                  P[5][:, :].bitcast(BF16)), ve, 1)
            bank_set(c, 5, ve)
        ve_vtok = ve.n

        if dbgw is not None and w == dbgw and l == 0:
            dq = dout(c, "dbg_qq", (128, 2048), BF16)
            dk = dout(c, "dbg_kr", (128, 4096), BF16)
            dv = dout(c, "dbg_vtok", (128, 4096), BF16)
            ve_now = ve.n

            @c.block.sync
            def _(e):
                e.wait_ge(ve.h, ve_now)
                inc(e.dma_start(out=dq[:, :], in_=qq[:, :]), dsem(c, "smisc"), 16)
                inc(e.dma_start(out=dk[:, :], in_=kr[:, :]), dsem(c, "smisc"), 16)
                inc(e.dma_start(out=dv[:, :], in_=vtok[:, :]), dsem(c, "smisc"), 16)

        jlist = list(range(8)) if w > 0 else list(range(4, 8))
        for h in range(8):
            off = 64 * (h % 2)
            pair = h // 2
            for it in range(4):
                @c.block.tensor
                def _(e):
                    bank_wait(c, e, 3)
                    bank_wait(c, e, 4)
                    if h == 0 and it == 0:
                        e.wait_ge(ve.h, ve_vtok)
                    lhs = qq[off:off + 64, pair * CH + it * 128:pair * CH + (it + 1) * 128]
                    if w > 0:
                        m0 = e.matmul(P[3][:, :], lhs,
                                      kr[off:off + 64, (pslot * 4 + pair) * CH:(pslot * 4 + pair + 1) * CH],
                                      start=True, stop=True)
                        inc(m0, pe, 1)
                    m1 = e.matmul(P[4][:, :], lhs,
                                  kr[off:off + 64, (slot * 4 + pair) * CH:(slot * 4 + pair + 1) * CH],
                                  start=True, stop=True)
                    inc(m1, pe, 1)
                pe_sim = pe.n

                @c.block.vector
                def _(e):
                    e.wait_ge(pe.h, pe_sim)
                    if w == 0 and h == 0 and it == 0:
                        e.wait_ge(dsem(c, 'sconst').h, c.sl_masks)
                    if c.sc_exp_last is not None:
                        e.wait_ge(sc.h, c.sc_exp_last)
                    if w > 0:
                        e.tensor_add(simf[:, 0:512], P[3][:, :], masks[:, (it * 8 + 0) * 128:(it * 8 + 4) * 128])
                    inc(e.tensor_add(simf[:, 512:1024], P[4][:, :], masks[:, (it * 8 + 4) * 128:(it * 8 + 8) * 128]), ve, 1)
                    e.wait_ge(ve.h, ve.n)
                    if w > 0:
                        inc(e.reduce_max(acol[:, 0:1], simf[:, :], axis=AX, negate=True), ve, 1)
                    else:
                        inc(e.reduce_max(acol[:, 0:1], simf[:, 512:1024], axis=AX, negate=True), ve, 1)
                bank_set(c, 3, ve)
                bank_set(c, 4, ve)
                ve_sm = ve.n

                @c.block.scalar
                def _(e):
                    e.wait_ge(ve.h, ve_sm)
                    if c.pe_tp_last is not None:
                        e.wait_ge(pe.h, c.pe_tp_last)
                    if w > 0:
                        e.activation(attn_s[:, 0:512], simf[:, 0:512], AF.Exp,
                                     bias=acol[:, 0:1], accum_out=acol[:, 2:3])
                    inc(e.activation(attn_s[:, 512:1024], simf[:, 512:1024], AF.Exp,
                                     bias=acol[:, 0:1], accum_out=acol[:, 3:4]), sc, 1)
                sc_exp = sc.n
                c.sc_exp_last = sc.n

                @c.block.vector
                def _(e):
                    e.wait_ge(sc.h, sc_exp)
                    if w > 0:
                        inc(e.tensor_add(acol[:, 2:3], acol[:, 2:3], acol[:, 3:4]), ve, 1)
                        e.wait_ge(ve.h, ve.n)
                        inc(e.reciprocal(acol[:, 4:5], acol[:, 2:3]), ve, 1)
                    else:
                        inc(e.reciprocal(acol[:, 4:5], acol[:, 3:4]), ve, 1)
                ve_den = ve.n

                @c.block.scalar
                def _(e):
                    e.wait_ge(ve.h, ve_den)
                    if w > 0:
                        inc(e.mul(attn_s[:, 0:1024], attn_s[:, 0:1024], acol[:, 4:5]), sc, 1)
                    else:
                        inc(e.mul(attn_s[:, 512:1024], attn_s[:, 512:1024], acol[:, 4:5]), sc, 1)
                sc_scale = sc.n

                if dbgw is not None and w == dbgw and l == 0 and h == 2 and it == 3:
                    dsim = dout(c, "dbg_simf", (128, 1024), F32)
                    datn = dout(c, "dbg_attn", (128, 1024), BF16)
                    dcol = dout(c, "dbg_acol", (128, 8), F32)

                    @c.block.sync
                    def _(e):
                        e.wait_ge(sc.h, sc_scale)
                        inc(e.dma_start(out=dsim[:, :], in_=simf[:, :]), dsem(c, "smisc"), 16)
                        inc(e.dma_start(out=datn[:, :], in_=attn_s[:, 0:1024]), dsem(c, "smisc"), 16)
                        inc(e.dma_start(out=dcol[:, :], in_=acol[:, :]), dsem(c, "smisc"), 16)

                @c.block.tensor
                def _(e):
                    bank_wait(c, e, 5)
                    e.wait_ge(sc.h, sc_scale)
                    if h == 0 and it == 0 and attnT_free is not None:
                        s, n = attnT_free
                        e.wait_ge(s.h, n)
                    m = None
                    for jt in jlist:
                        m = e.transpose(P[5][:, :].bitcast(BF16)[:, (jt % 8) * 128:(jt % 8 + 1) * 128],
                                        attn_s[:, (jt - (0 if w > 0 else 4)) * 128:(jt - (0 if w > 0 else 4) + 1) * 128]
                                        if False else attn_s[:, jt * 128:(jt + 1) * 128],
                                        c.idb[:, :])
                    inc(m, pe, 1)
                pe_tp = pe.n
                c.pe_tp_last = pe_tp

                @c.block.vector
                def _(e):
                    e.wait_ge(pe.h, pe_tp)
                    at = attnT[:, :].rearrange("p (jt i) -> p jt i", jt=8)
                    m = None
                    for jt in jlist:
                        m = e.tensor_copy(at[:, jt, it * 128:(it + 1) * 128],
                                          P[5][:, :].bitcast(BF16)[:, jt * 128:(jt + 1) * 128])
                    inc(m, ve, 1)
                bank_set(c, 5, ve)
            ve_at = ve.n

            @c.block.tensor
            def _(e):
                bank_wait(c, e, 6)
                e.wait_ge(ve.h, ve_at)
                m = None
                for jt in jlist:
                    sslot = pslot if jt < 4 else slot
                    jtl = jt % 4
                    m = e.matmul(P[6][0:64, :],
                                 vtok[:, sslot * 2048 + (pair * 4 + jtl) * 128 + off:
                                      sslot * 2048 + (pair * 4 + jtl) * 128 + off + 64],
                                 attnT[:, jt * 512:(jt + 1) * 512],
                                 start=(jt == jlist[0]), stop=(jt == jlist[-1]))
                inc(m, pe, 1)
            pe_pv = pe.n

            if off == 0:
                @c.block.vector
                def _(e):
                    e.wait_ge(pe.h, pe_pv)
                    inc(e.tensor_copy(oTw[0:64, pair * CH:(pair + 1) * CH], P[6][0:64, :]), ve, 1)
                bank_set(c, 6, ve)
            else:
                @c.block.vector
                def _(e):
                    e.wait_ge(pe.h, pe_pv)
                    inc(e.tensor_copy(o64b[0:64, :], P[6][0:64, :]), ve, 1)
                bank_set(c, 6, ve)
                ve_o = ve.n

                @c.block.tensor
                def _(e):
                    bank_wait(c, e, 7)
                    e.wait_ge(ve.h, ve_o)
                    inc(e.matmul(P[7][:, :], c.shift64[:, :], o64b[0:64, :], start=True, stop=True), pe, 1)
                pe_sh = pe.n

                @c.block.vector
                def _(e):
                    e.wait_ge(pe.h, pe_sh)
                    inc(e.tensor_copy(oTw[64:128, pair * CH:(pair + 1) * CH], P[7][64:128, :]), ve, 1)
                bank_set(c, 7, ve)
        if dbgw is not None and w == dbgw and l == 0:
            do_ = dout(c, "dbg_oTw", (128, 2048), BF16)
            ve_now2 = ve.n

            @c.block.sync
            def _(e):
                e.wait_ge(ve.h, ve_now2)
                inc(e.dma_start(out=do_[:, :], in_=oTw[:, :]), dsem(c, "smisc"), 16)
        qq_free = (pe, pe.n)
        kr_free[slot] = None
        kr_free[pslot] = (pe, pe.n)
        kr_free[slot] = (pe, pe.n)
        vtok_free[pslot] = (pe, pe.n)
        vtok_free[slot] = (pe, pe.n)
        vt_free = (pe, pe.n)
        attnT_free = (pe, pe.n)
        ve_oTw = ve.n

        for ot in range(NC):
            b = ot % 2
            wslot = ot % 2

            @c.block.scalar
            def _(e):
                if w == 0 and ot == 0:
                    e.wait_ge(dsem(c, 'sconv1').h, c.gl_qkvconv)
                if l > 0 and w == 0 and ot < 2:
                    e.wait_ge(pe.h, pe_prev)
                    e.wait_ge(ve.h, ve_prev)
                if c.fwostr_free[wslot] is not None:
                    s, n = c.fwostr_free[wslot]
                    e.wait_ge(s.h, n)
                inc(e.dma_start(out=fwostr[:, wslot * 512:(wslot + 1) * 512]
                                .rearrange("p (a f) -> p a f", a=4),
                                in_=c.dram[f"fwob{l}"][:, ot * 128:(ot + 1) * 128]
                                .rearrange("(a p) f -> p a f", p=128)), dsem(c, f"sfwo{wslot}"), 16)
            s2_wo = dsem(c, f"sfwo{wslot}").n

            @c.block.tensor
            def _(e):
                bank_wait(c, e, b)
                e.wait_ge(dsem(c, f"sfwo{wslot}").h, s2_wo)
                if ot == 0:
                    e.wait_ge(ve.h, ve_oTw)
                m = None
                for ci in range(4):
                    m = e.matmul(P[b][:, :], fwostr[:, wslot * 512 + ci * 128:wslot * 512 + (ci + 1) * 128],
                                 oTw[:, ci * CH:(ci + 1) * CH], start=(ci == 0), stop=(ci == 3))
                inc(m, pe, 1)
            pe_wo = pe.n
            c.fwostr_free[wslot] = (pe, pe.n)

            @c.block.vector
            def _(e):
                e.wait_ge(pe.h, pe_wo)
                e.tensor_copy(stash[:, ot:ot + 1], c.xT[ot][:, t0 + CH - 1:t0 + CH])
                inc(e.scalar_tensor_tensor(c.xT[ot][:, t0:t0 + CH], P[b][:, :], 1.0,
                                           c.xT[ot][:, t0:t0 + CH], ALU.mult, ALU.add), ve, 1)
            bank_set(c, b, ve)
        veA_win.append(ve.n)

    return veA_win


def emit_l0ffn_ffn(c, l, alloc, veA_win, dbgc=None):
    nc, P = c.nc, c.P
    gl, sl, s2, pe, ve, sc = c.gl, c.sl, c.s2, c.pe, c.ve, c.sc
    peA, scA = pe.n, sc.n

    @c.block.vector
    def _(e):
        e.wait_ge(pe.h, peA)
        e.wait_ge(sc.h, scA)
        e.wait_ge(ve.h, veA_win[-1])
        inc(e.memset(c.cpool[:, 6:7], 0.0), ve, 1)
    c.ffn_barrier = ve.n
    c.lnsq = alloc(f"f{l}lnsq", [128, 2 * CH], F32)
    c.lnrow = alloc(f"f{l}lnrow", [1, 2 * CH], F32)
    lnh = [alloc(f"f{l}lnh{i}", [128, CH], BF16) for i in range(NC)]
    hh = alloc(f"f{l}hh", [128, NFK * CH], BF16)
    w2cb = alloc(f"f{l}w2cb", [128, NFK * 256], BF16)
    wab = alloc(f"f{l}wab", [128, 2 * 2048], BF16)
    wgb = alloc(f"f{l}wgb", [128, 2 * 2048], BF16)
    shz2 = alloc(f"f{l}shz2", [128, 4 * CH], BF16)
    gel = alloc(f"f{l}gel", [128, 2 * CH], F32)
    stf = alloc(f"f{l}stf", [128, NC], BF16)
    smalls = {}
    for nm in (f"lng{l}", f"lnb{l}", f"b1a{l}", f"b1g{l}", f"b2c{l}"):
        smalls[nm] = alloc("s_" + nm, [128, c.dram[nm].shape[1]], F32)

    sco = dsem(c, "sconst")

    @c.block.sync
    def _(e):
        e.wait_ge(ve.h, c.ffn_barrier)
        e.wait_ge(sco.h, sco.n)
        m = None
        for nm, t in smalls.items():
            m = inc(e.dma_start(out=t[:, :], in_=c.dram[nm][:, :]), sco, 16)
    sl_smalls = sco.n

    @c.block.vector
    def _(e):
        inc(e.memset(stf[:, :], 0.0), ve, 1)
    ve_stf0 = ve.n

    GRPS = [(k, 2) for k in range(0, NFK, 2)]
    wab_free = [None, None]
    w2cb_free = None
    hh_free = None
    shz2_free = None

    for chn in range(NCH):
        t0 = chn * CH
        @c.block.vector
        def _(e):
            e.wait_ge(ve.h, veA_win[chn])
            if chn == 0:
                e.wait_ge(ve.h, ve_stf0)
            if shz2_free is not None:
                s, n = shz2_free
                e.wait_ge(s.h, n)
            m = None
            for j in range(4):
                m = e.tensor_copy(shz2[:, j * CH:j * CH + 1], stf[:, 4 + j:5 + j])
                m = e.tensor_copy(shz2[:, j * CH + 1:j * CH + CH],
                                  c.xT[4 + j][:, t0:t0 + CH - 1])
            inc(m, ve, 1)
        ve_sh2 = ve.n

        srcs = [c.xT[i][:, t0:t0 + CH] for i in range(4)] +                [shz2[:, j * CH:(j + 1) * CH] for j in range(4)]
        lng = smalls[f"lng{l}"]
        lnb = smalls[f"lnb{l}"]

        def wr(ot, tmp, e, chn=chn):
            e.tensor_mul(tmp, tmp, c.P[4][:, 0:CH])
            if hh_free is not None and ot == 0:
                s, n = hh_free
                e.wait_ge(s.h, n)
            inc(e.tensor_scalar(lnh[ot][:, :], tmp, lng[:, ot:ot + 1], lnb[:, ot:ot + 1],
                                ALU.mult, ALU.add), ve, 1)

        if chn == 0:
            @c.block.vector
            def _(e):
                e.wait_ge(dsem(c, "sconst").h, sl_smalls)
            @c.block.scalar
            def _(e):
                e.wait_ge(dsem(c, "sconst").h, sl_smalls)
        ready = [(ve, ve_sh2)] * NC
        emit_ln2(c, srcs, CH, wr, ready)
        ve_lnh = ve.n

        for gi, (k0, nk) in enumerate(GRPS):
            slot = gi % 2
            cols = 256

            if dbgc is not None and chn == 0 and l == 0 and gi == 2:
                dwa = dout(c, "dbg_wab", (128, 4096), BF16)
                dwg = dout(c, "dbg_wgb", (128, 4096), BF16)
                sl_wprev = dsem(c, "swab1").n

                @c.block.sync
                def _(e):
                    e.wait_ge(dsem(c, "swab1").h, sl_wprev)
                    inc(e.dma_start(out=dwa[:, :], in_=wab[:, :]), dsem(c, "smisc"), 16)
                    inc(e.dma_start(out=dwg[:, :], in_=wgb[:, :]), dsem(c, "smisc"), 16)
                dbg_gate = dsem(c, "smisc").n
            else:
                dbg_gate = None

            @c.block.sync
            def _(e):
                if chn == 0 and gi == 0:
                    e.wait_ge(ve.h, c.ffn_barrier)
                if dbg_gate is not None:
                    e.wait_ge(dsem(c, "smisc").h, dbg_gate)
                e.wait_ge(dsem(c, 'sconv2').h, c.gl_conv)
                if wab_free[slot] is not None:
                    s, n = wab_free[slot]
                    e.wait_ge(s.h, n)
                sw = dsem(c, f"swab{slot}")
                inc(e.dma_start(out=wab[:, slot * 2048:slot * 2048 + 8 * cols]
                                .rearrange("p (a f) -> p a f", a=8),
                                in_=c.dram[f"w1ab{l}"][:, k0 * 128:k0 * 128 + cols]
                                .rearrange("(a p) f -> p a f", p=128)), sw, 16)
                inc(e.dma_start(out=wgb[:, slot * 2048:slot * 2048 + 8 * cols]
                                .rearrange("p (a f) -> p a f", a=8),
                                in_=c.dram[f"w1gb{l}"][:, k0 * 128:k0 * 128 + cols]
                                .rearrange("(a p) f -> p a f", p=128)), sw, 16)
            sl_w = dsem(c, f"swab{slot}").n

            for kk in range(nk):
                kidx = k0 + kk
                fk = FKS[kidx]
                koff = sum(FKS[k0:k0 + kk])
                ba = 2 * (kidx % 2)
                bg_ = ba + 1

                @c.block.tensor
                def _(e):
                    bank_wait(c, e, ba)
                    bank_wait(c, e, bg_)
                    e.wait_ge(dsem(c, f"swab{slot}").h, sl_w)
                    if kidx == 0:
                        e.wait_ge(ve.h, ve_lnh)
                    m = None
                    for a in range(NC):
                        m = e.matmul(P[ba][0:fk, :],
                                     wab[:, slot * 2048 + a * cols + koff:slot * 2048 + a * cols + koff + fk],
                                     lnh[a][:, :], start=(a == 0), stop=(a == NC - 1))
                    inc(m, pe, 1)
                    m = None
                    for a in range(NC):
                        m = e.matmul(P[bg_][0:fk, :],
                                     wgb[:, slot * 2048 + a * cols + koff:slot * 2048 + a * cols + koff + fk],
                                     lnh[a][:, :], start=(a == 0), stop=(a == NC - 1))
                    inc(m, pe, 1)
                pe_w1 = pe.n
                if kidx == NFK - 1:
                    wab_free[slot] = None

                @c.block.scalar
                def _(e):
                    e.wait_ge(pe.h, pe_w1)
                    inc(e.activation(gel[0:fk, (kidx % 2) * CH:(kidx % 2) * CH + CH],
                                     P[bg_][0:fk, :], AF.Gelu,
                                     bias=smalls[f"b1g{l}"][0:fk, kidx:kidx + 1]), sc, 1)
                bank_set(c, bg_, sc)
                sc_g = sc.n

                @c.block.vector
                def _(e):
                    e.wait_ge(sc.h, sc_g)
                    inc(e.scalar_tensor_tensor(hh[0:fk, kidx * CH:(kidx + 1) * CH],
                                               P[ba][0:fk, :], smalls[f"b1a{l}"][0:fk, kidx:kidx + 1],
                                               gel[0:fk, (kidx % 2) * CH:(kidx % 2) * CH + CH],
                                               ALU.add, ALU.mult), ve, 1)
                bank_set(c, ba, ve)
            wab_free[slot] = (pe, pe.n)
        ve_hh = ve.n

        if dbgc is not None and chn == 0 and l == 0:
            dl = dout(c, "dbg_lnh", (128, 8 * CH), BF16)
            dh = dout(c, "dbg_hh", (128, NFK * CH), BF16)
            dsh = dout(c, "dbg_shz2", (128, 4 * CH), BF16)

            @c.block.sync
            def _(e):
                e.wait_ge(ve.h, ve_hh)
                for i in range(NC):
                    inc(e.dma_start(out=dl[:, i * CH:(i + 1) * CH], in_=lnh[i][:, :]), dsem(c, "smisc"), 16)
                inc(e.dma_start(out=dh[:, :], in_=hh[:, :]), dsem(c, "smisc"), 16)
                inc(e.dma_start(out=dsh[:, :], in_=shz2[:, :]), dsem(c, "smisc"), 16)

        for qtr in range(4):
            @c.block.scalar
            def _(e):
                if chn == 0 and qtr == 0:
                    e.wait_ge(ve.h, c.ffn_barrier)
                e.wait_ge(dsem(c, 'sconv2').h, c.gl_conv)
                if w2cb_free is not None:
                    s, n = w2cb_free
                    e.wait_ge(s.h, n)
                sw2 = dsem(c, "sw2cb")
                inc(e.dma_start(out=w2cb[:, 0:21 * 256].rearrange("p (a f) -> p a f", a=21),
                                in_=c.dram[f"w2b{l}"][0:2688, qtr * 256:(qtr + 1) * 256]
                                .rearrange("(a p) f -> p a f", p=128)), sw2, 16)
                inc(e.dma_start(out=w2cb[0:42, 21 * 256:21 * 256 + 256],
                                in_=c.dram[f"w2b{l}"][2688:2730, qtr * 256:(qtr + 1) * 256]), sw2, 16)
            s2_w2 = dsem(c, "sw2cb").n

            for ot4 in range(2):
                ot = qtr * 2 + ot4
                bnk = 4 + ot4

                @c.block.tensor
                def _(e):
                    bank_wait(c, e, bnk)
                    e.wait_ge(dsem(c, "sw2cb").h, s2_w2)
                    if qtr == 0 and ot4 == 0:
                        e.wait_ge(ve.h, ve_hh)
                    m = None
                    for kidx in range(NFK):
                        fk = FKS[kidx]
                        m = e.matmul(P[bnk][:, :],
                                     w2cb[0:fk, kidx * 256 + ot4 * 128:kidx * 256 + (ot4 + 1) * 128],
                                     hh[0:fk, kidx * CH:(kidx + 1) * CH],
                                     start=(kidx == 0), stop=(kidx == NFK - 1))
                    inc(m, pe, 1)
                pe_y = pe.n

                @c.block.scalar
                def _(e):
                    e.wait_ge(pe.h, pe_y)
                    inc(e.add(P[bnk][:, :], P[bnk][:, :], smalls[f"b2c{l}"][:, ot:ot + 1]), sc, 1)
                sc_b2 = sc.n

                @c.block.vector
                def _(e):
                    e.wait_ge(sc.h, sc_b2)
                    e.tensor_copy(stf[:, ot:ot + 1], c.xT[ot][:, t0 + CH - 1:t0 + CH])
                    inc(e.scalar_tensor_tensor(c.xT[ot][:, t0:t0 + CH], P[bnk][:, :], 1.0,
                                               c.xT[ot][:, t0:t0 + CH], ALU.mult, ALU.add), ve, 1)
                bank_set(c, bnk, ve)
            w2cb_free = (pe, pe.n)
        hh_free = (pe, pe.n)
        shz2_free = (pe, pe.n)


def emit_phase2(c, p2stop=None, dbgw=None):
    nc, P = c.nc, c.P
    gl, sl, s2, pe, ve, sc = c.gl, c.sl, c.s2, c.pe, c.ve, c.sc
    phase_cms = []

    def alloc(name, shape, dtype):
        cm = nc.sbuf_tensor(name, shape, dtype)
        phase_cms.append(cm)
        return cm.__enter__()

    sconv1 = dsem(c, "sconv1")
    sconv2 = dsem(c, "sconv2")

    @c.block.gpsimd
    def _(e):
        for l in range(2):
            inc(e.dma_start(out=c.dram[f"qkvb{l}"][:, :], in_=c.dram[f"qkvw{l}"][:, :]), sconv1, 16)
            inc(e.dma_start(out=c.dram[f"fwob{l}"][:, :], in_=c.dram[f"fwo{l}"][:, :]), sconv1, 16)
    c.gl_qkvconv = sconv1.n

    @c.block.gpsimd
    def _(e):
        for l in range(2):
            for j in range(4):
                inc(e.dma_start(out=c.dram[f"w1ab{l}"][j * 256:(j + 1) * 256, :],
                                in_=c.dram[f"w1a_{l}"][j * 256:(j + 1) * 256, :]), sconv2, 16)
                inc(e.dma_start(out=c.dram[f"w1gb{l}"][j * 256:(j + 1) * 256, :],
                                in_=c.dram[f"w1g_{l}"][j * 256:(j + 1) * 256, :]), sconv2, 16)
            for j in range(2):
                lo = j * 1365
                hi = 2730 if j else 1365
                inc(e.dma_start(out=c.dram[f"w2b{l}"][lo:hi, :],
                                in_=c.dram[f"w2_{l}"][lo:hi, :]), sconv2, 16)
    c.gl_conv = sconv2.n

    c.blk01 = alloc("p2blk01", [128, 2], F32)
    c.expand2 = alloc("p2expand2", [2, 128], F32)
    c.S64 = alloc("p2S64", [128, 128], F32)
    c.dramcols = {}
    for l in range(2):
        for nm in (f"qs{l}", f"ks{l}"):
            t = alloc("s_" + nm, [128, 1], F32)
            c.dramcols[nm] = t[:, :]

    sco = dsem(c, "sconst")

    @c.block.sync
    def _(e):
        e.wait_ge(sco.h, sco.n)
        inc(e.dma_start(out=c.blk01[:, :], in_=c.dram["blk01"][:, :]), sco, 16)
        inc(e.dma_start(out=c.expand2[:, :], in_=c.dram["expand2"][:, :]), sco, 16)
        inc(e.dma_start(out=c.S64[:, :], in_=c.dram["S64"][:, :]), sco, 16)
        for l in range(2):
            for nm in (f"qs{l}", f"ks{l}"):
                inc(e.dma_start(out=c.dramcols[nm], in_=c.dram[nm][:, :]), sco, 16)
    sl_c2 = sco.n

    @c.block.tensor
    def _(e):
        e.wait_ge(sco.h, sl_c2)

    for l in range(2):
        layer_cms = []

        def lalloc(name, shape, dtype):
            cm = nc.sbuf_tensor(name, shape, dtype)
            layer_cms.append(cm)
            return cm.__enter__()

        masks = lalloc(f"a{l}masks", [128, 4096], BF16)
        qkvstr = lalloc(f"a{l}qkvstr", [128, 2 * 1024], BF16)
        fwostr = lalloc(f"a{l}fwostr", [128, 2 * 512], BF16)
        shz = lalloc(f"a{l}shz", [128, 4 * CH], BF16)
        qq = lalloc(f"a{l}qq", [128, 4 * CH], BF16)
        kr = lalloc(f"a{l}kr", [128, 8 * CH], BF16)
        vt = lalloc(f"a{l}vt", [128, 4 * CH], BF16)
        vtok = lalloc(f"a{l}vtok", [128, 2 * 2048], BF16)
        attnT = lalloc(f"a{l}attnT", [128, 4096], BF16)
        oTw = lalloc(f"a{l}oTw", [128, 4 * CH], BF16)
        o64b = lalloc(f"a{l}o64", [64, CH], BF16)
        pscr = lalloc(f"a{l}pscr", [128, 1024], F32)
        sq1 = lalloc(f"a{l}sq1", [128, 1024], F32)
        nrow = lalloc(f"a{l}nrow", [2, 512], F32)
        rn2 = lalloc(f"a{l}rn2", [2, 512], F32)
        tabs = attnT[:, 0:2048].bitcast(F32)
        simf = pscr[:, :]
        attn_s = sq1[:, :].bitcast(BF16)
        acol = lalloc(f"a{l}acol", [128, 8], F32)
        stash = lalloc(f"a{l}stash", [128, 8], BF16)
        bufs = (qkvstr, fwostr, shz, qq, kr, vt, vtok, attnT, oTw, o64b,
                tabs, pscr, sq1, nrow, rn2, simf, attn_s, acol, stash)
        veA_win = emit_l0ffn_layer(c, l, lalloc, masks, bufs, dbgw=dbgw)
        for cm in reversed(layer_cms):
            cm.__exit__(None, None, None)
        layer_cms.clear()
        if p2stop == f"za{l}":
            for cm in reversed(phase_cms):
                cm.__exit__(None, None, None)
            return
        emit_l0ffn_ffn(c, l, lalloc, veA_win, dbgc=dbgw)
        for cm in reversed(layer_cms):
            cm.__exit__(None, None, None)
        layer_cms.clear()
        if p2stop == f"zf{l}":
            for cm in reversed(phase_cms):
                cm.__exit__(None, None, None)
            return

    pe_o, sc_o = c.pe.n, c.sc.n

    @c.block.vector
    def _(e):
        e.wait_ge(c.pe.h, pe_o)
        e.wait_ge(c.sc.h, sc_o)
        inc(e.memset(c.cpool[:, 6:7], 0.0), c.ve, 1)
    outer_barrier = c.ve.n
    c.lnsq = alloc("p2lnsq", [128, 2 * CH], F32)
    c.lnrow = alloc("p2lnrow", [1, 2 * CH], F32)
    x1t = [alloc(f"p2x1t{i}", [128, CH], BF16) for i in range(NC)]
    rbuf = [alloc(f"p2rb{i}", [128, CH], F32) for i in range(NC)]
    x1_free = None
    for chn in range(NCH):
        t0 = chn * CH

        @c.block.scalar
        def _(e):
            if chn == 0:
                e.wait_ge(c.ve.h, outer_barrier)
                e.wait_ge(dsem(c, "sx1w").h, dsem(c, "sx1w").n)
            if x1_free is not None:
                s, n = x1_free
                e.wait_ge(s.h, n)
            m = None
            for i in range(NC):
                m = inc(e.dma_start(out=x1t[i][:, :],
                                    in_=c.dram["x1s"][i * 128:(i + 1) * 128, t0:t0 + CH]), dsem(c, "sx1r"), 16)
        s2_x1 = dsem(c, "sx1r").n

        ve_ready = []
        for ot in range(NC):
            @c.block.vector
            def _(e):
                e.wait_ge(dsem(c, "sx1r").h, s2_x1)
                inc(e.tensor_add(rbuf[ot][:, :], c.xT[ot][:, t0:t0 + CH], x1t[ot][:, :]), ve, 1)
            ve_ready.append((ve, ve.n))
        x1_free = (ve, ve.n)

        def wr(ot, tmp, e, t0=t0):
            inc(e.tensor_mul(c.xT[ot][:, t0:t0 + CH], tmp, c.P[4][:, 0:CH]), ve, 1)

        emit_ln2(c, rbuf, CH, wr, ve_ready)

    for cm in reversed(phase_cms):
        cm.__exit__(None, None, None)




def phase3_inputs(c):
    din(c, "l1wq", (D, 512), F32)
    din(c, "l1wk", (D, 512), F32)
    din(c, "l1wv", (D, D), F32)
    din(c, "l1ws", (D, 256), F32)
    din(c, "l1wg", (D, D), F32)
    din(c, "l1wo", (D, D), F32)
    din(c, "qcw3", (128, 16), F32)
    din(c, "kcw3", (128, 16), F32)
    din(c, "vcw3", (128, 32), F32)
    din(c, "gnw3", (128, 2), F32)
    din(c, "S128", (128, 128), F32)
    din(c, "triu_b", (128, 128), BF16)
    for nm in ("rcosq", "rsinq", "rcos", "rsin"):
        din(c, nm, (128, T), F32)
    dscratch(c, "q3s", (512, T), BF16)
    dscratch(c, "k3s", (512, T), BF16)
    dscratch(c, "v3s", (D, T), BF16)
    dscratch(c, "g3s", (D, T), BF16)
    dscratch(c, "e3s", (64, 4 * T), F32)


def phase3_host(h, inp):
    h["l1wq"] = np.ascontiguousarray(inp["L1_attn_Wq"])
    h["l1wk"] = np.ascontiguousarray(inp["L1_attn_Wk"])
    h["l1wv"] = np.ascontiguousarray(inp["L1_attn_Wv"])
    h["l1ws"] = np.ascontiguousarray(inp["L1_attn_Ws"])
    h["l1wg"] = np.ascontiguousarray(inp["L1_attn_Wg"])
    h["l1wo"] = np.ascontiguousarray(inp["L1_attn_Wo"])
    for nm, key, nt in (("qcw3", "L1_attn_qconv_w", 4), ("kcw3", "L1_attn_kconv_w", 4),
                        ("vcw3", "L1_attn_vconv_w", 8)):
        w = inp[key][:, 0, :]
        t = np.zeros((128, nt * 4), np.float32)
        for j in range(nt):
            t[:, j * 4:(j + 1) * 4] = w[j * 128:(j + 1) * 128, :]
        h[nm] = t
    h["gnw3"] = np.ascontiguousarray(inp["L1_attn_gnorm_w"].reshape(2, 128).T)
    S = np.zeros((128, 128), np.float32)
    for m in range(64):
        S[m + 64, m] = -1.0
        S[m, m + 64] = 1.0
    h["S128"] = S
    tri = (np.arange(128)[:, None] <= np.arange(128)[None, :]).astype(np.float32)
    h["triu_b"] = tri.astype(BF)
    hk = 128
    invf = 1.0 / 10000.0 ** (np.arange(0, hk, 2, dtype=np.float64) / hk)
    fr = np.arange(T, dtype=np.float64)[:, None] * invf[None, :]
    emb = np.concatenate([fr, fr], -1)
    cos = np.cos(emb).T
    sin = np.sin(emb).T
    sc = hk ** (-0.5)
    h["rcosq"] = np.ascontiguousarray((cos * sc).astype(np.float32))
    h["rsinq"] = np.ascontiguousarray((sin * sc).astype(np.float32))
    h["rcos"] = np.ascontiguousarray(cos.astype(np.float32))
    h["rsin"] = np.ascontiguousarray(sin.astype(np.float32))


def emit_phase3(c):
    nc, P = c.nc, c.P
    pe, ve, sc = c.pe, c.ve, c.sc
    phase_cms = []

    def alloc(name, shape, dtype):
        cm = nc.sbuf_tensor(name, shape, dtype)
        phase_cms.append(cm)
        return cm.__enter__()

    peB, scB = pe.n, sc.n

    @c.block.vector
    def _(e):
        e.wait_ge(pe.h, peB)
        e.wait_ge(sc.h, scB)
        inc(e.memset(c.cpool[:, 6:7], 0.0), ve, 1)
    bar3 = ve.n

    sw3 = dsem(c, "sw3")
    st3a = dsem(c, "st3a")
    stab3 = dsem(c, "stab3")
    sco = dsem(c, "sconst")

    a_cms = []

    def aalloc(name, shape, dtype):
        cm = nc.sbuf_tensor(name, shape, dtype)
        a_cms.append(cm)
        return cm.__enter__()

    wq_sb = aalloc("p3wq", [128, 4096], BF16)
    wk_sb = aalloc("p3wk", [128, 4096], BF16)
    wv_sb = aalloc("p3wv", [128, 8192], BF16)
    ws_sb = aalloc("p3ws", [128, 2048], BF16)
    wg_sb = aalloc("p3wg", [128, 8192], BF16)
    ccols = aalloc("p3ccols", [128, 64], F32)
    tab3 = aalloc("p3tab", [128, 2048], F32)
    pp = aalloc("p3pp", [128, 520], F32)
    st3 = aalloc("p3st3", [128, 48], F32)
    qc = aalloc("p3qc", [128, 512], F32)
    tmp1 = aalloc("p3tmp1", [128, 512], F32)
    tmp2 = aalloc("p3tmp2", [128, 512], F32)
    obf = aalloc("p3obf", [128, 512], BF16)
    S128s = aalloc("p3S128", [128, 128], F32)

    @c.block.gpsimd
    def _(e):
        e.wait_ge(ve.h, bar3)
        for wsb, nm in ((wq_sb, "l1wq"), (wk_sb, "l1wk"), (wv_sb, "l1wv"),
                        (ws_sb, "l1ws"), (wg_sb, "l1wg")):
            inc(e.dma_start(out=wsb[:, :].rearrange("p (a f) -> p a f", a=8),
                            in_=c.dram[nm][:, :].rearrange("(a p) f -> p a f", p=128)), sw3, 16)
    sw3_n = sw3.n

    @c.block.sync
    def _(e):
        e.wait_ge(ve.h, bar3)
        e.wait_ge(sco.h, sco.n)
        inc(e.dma_start(out=ccols[:, 0:16], in_=c.dram["qcw3"][:, :]), sco, 16)
        inc(e.dma_start(out=ccols[:, 16:32], in_=c.dram["kcw3"][:, :]), sco, 16)
        inc(e.dma_start(out=ccols[:, 32:64], in_=c.dram["vcw3"][:, :]), sco, 16)
        inc(e.dma_start(out=S128s[:, :], in_=c.dram["S128"][:, :]), sco, 16)
    sco_3a = sco.n

    @c.block.vector
    def _(e):
        inc(e.memset(st3[:, :], 0.0), ve, 1)
    ve_st0 = ve.n

    def conv_into_tmp1(e, wbase, tile_idx):
        e.tensor_copy(pp[:, 0:3], st3[:, tile_idx * 3:tile_idx * 3 + 3])
        e.tensor_scalar_mul(tmp1[:, :], pp[:, 3:515], ccols[:, wbase + 3:wbase + 4])
        e.scalar_tensor_tensor(tmp1[:, :], pp[:, 2:514], ccols[:, wbase + 2:wbase + 3],
                               tmp1[:, :], ALU.mult, ALU.add)
        e.scalar_tensor_tensor(tmp1[:, :], pp[:, 1:513], ccols[:, wbase + 1:wbase + 2],
                               tmp1[:, :], ALU.mult, ALU.add)
        e.scalar_tensor_tensor(tmp1[:, :], pp[:, 0:512], ccols[:, wbase + 0:wbase + 1],
                               tmp1[:, :], ALU.mult, ALU.add)
        inc(e.tensor_copy(st3[:, tile_idx * 3:tile_idx * 3 + 3], pp[:, 512:515]), ve, 1)

    tab3_free = None
    obf_st = None
    tmp1_st = None
    sil_prev = 0
    for chn in range(NCH):
        t0 = chn * CH

        @c.block.sync
        def _(e):
            if tab3_free is not None:
                s, n = tab3_free
                e.wait_ge(s.h, n)
            for j, nm in enumerate(("rcosq", "rsinq", "rcos", "rsin")):
                inc(e.dma_start(out=tab3[:, j * 512:(j + 1) * 512],
                                in_=c.dram[nm][:, t0:t0 + CH]), stab3, 16)
        sl_tab3 = stab3.n

        for kind, wsb, wb0, scr, coso in (("q", wq_sb, 0, "q3s", 0), ("k", wk_sb, 16, "k3s", 1024)):
            for ht in range(4):
                b = ht % 2
                wbase = wb0 + 4 * ht
                tile_idx = (0 if kind == "q" else 4) + ht

                @c.block.tensor
                def _(e):
                    bank_wait(c, e, b)
                    if chn == 0 and kind == "q" and ht == 0:
                        e.wait_ge(sw3.h, sw3_n)
                        e.wait_ge(ve.h, ve_st0)
                    m = None
                    for ci in range(NC):
                        m = e.matmul(P[b][:, :], wsb[:, ci * 512 + ht * 128:ci * 512 + (ht + 1) * 128],
                                     c.xT[ci][:, t0:t0 + CH], start=(ci == 0), stop=(ci == NC - 1))
                    inc(m, pe, 1)
                pe_pr = pe.n

                @c.block.vector
                def _(e):
                    e.wait_ge(pe.h, pe_pr)
                    if chn == 0 and kind == "q" and ht == 0:
                        e.wait_ge(sco.h, sco_3a)
                    if tmp1_st is not None:
                        e.wait_ge(st3a.h, tmp1_st)
                    if sil_prev:
                        e.wait_ge(sc.h, sil_prev)
                    inc(e.tensor_copy(pp[:, 3:515], P[b][:, :]), ve, 1)
                    conv_into_tmp1(e, wbase, tile_idx)
                bank_set(c, b, ve)
                ve_cv = ve.n

                @c.block.scalar
                def _(e):
                    e.wait_ge(ve.h, ve_cv)
                    inc(e.activation(qc[:, :], tmp1[:, :], AF.Silu), sc, 1)
                sc_si = sc.n
                sil_prev = sc.n

                @c.block.tensor
                def _(e):
                    bank_wait(c, e, 2)
                    e.wait_ge(sc.h, sc_si)
                    inc(e.matmul(P[2][:, :], S128s[:, :], qc[:, :], start=True, stop=True), pe, 1)
                pe_rot = pe.n

                @c.block.vector
                def _(e):
                    e.wait_ge(pe.h, pe_rot)
                    if ht == 0 and kind == "q":
                        e.wait_ge(stab3.h, sl_tab3)
                    if obf_st is not None:
                        e.wait_ge(st3a.h, obf_st)
                    e.tensor_mul(tmp2[:, :], qc[:, :], tab3[:, coso:coso + 512])
                    e.scalar_tensor_tensor(pp[:, 0:512], P[2][:, :], 1.0,
                                           tab3[:, coso + 512:coso + 1024], ALU.mult, ALU.mult)
                    inc(e.tensor_add(obf[:, :], tmp2[:, :], pp[:, 0:512]), ve, 1)
                bank_set(c, 2, ve)
                ve_rp = ve.n

                @c.block.sync
                def _(e):
                    e.wait_ge(ve.h, ve_rp)
                    inc(e.dma_start(out=c.dram[scr][ht * 128:(ht + 1) * 128, t0:t0 + CH],
                                    in_=obf[:, :]), st3a, 16)
                obf_st = st3a.n
        for vt in range(NC):
            b = vt % 2
            tile_idx = 8 + vt

            @c.block.tensor
            def _(e):
                bank_wait(c, e, b)
                m = None
                for ci in range(NC):
                    m = e.matmul(P[b][:, :], wv_sb[:, ci * 1024 + vt * 128:ci * 1024 + (vt + 1) * 128],
                                 c.xT[ci][:, t0:t0 + CH], start=(ci == 0), stop=(ci == NC - 1))
                inc(m, pe, 1)
            pe_pr = pe.n

            @c.block.vector
            def _(e):
                e.wait_ge(pe.h, pe_pr)
                if sil_prev:
                    e.wait_ge(sc.h, sil_prev)
                inc(e.tensor_copy(pp[:, 3:515], P[b][:, :]), ve, 1)
                conv_into_tmp1(e, 32 + 4 * vt, tile_idx)
            bank_set(c, b, ve)
            ve_cv = ve.n

            @c.block.scalar
            def _(e):
                e.wait_ge(ve.h, ve_cv)
                if obf_st is not None:
                    e.wait_ge(st3a.h, obf_st)
                inc(e.activation(obf[:, :], tmp1[:, :], AF.Silu), sc, 1)
            sc_si = sc.n
            sil_prev = sc.n

            @c.block.sync
            def _(e):
                e.wait_ge(sc.h, sc_si)
                inc(e.dma_start(out=c.dram["v3s"][vt * 128:(vt + 1) * 128, t0:t0 + CH],
                                in_=obf[:, :]), st3a, 16)
            obf_st = st3a.n
        for h in range(4):
            b = 2 + (h % 2)

            @c.block.tensor
            def _(e):
                bank_wait(c, e, b)
                m = None
                for ci in range(NC):
                    m = e.matmul(P[b][0:64, :], ws_sb[:, ci * 256 + h * 64:ci * 256 + (h + 1) * 64],
                                 c.xT[ci][:, t0:t0 + CH], start=(ci == 0), stop=(ci == NC - 1))
                inc(m, pe, 1)
            pe_pr = pe.n

            @c.block.vector
            def _(e):
                e.wait_ge(pe.h, pe_pr)
                inc(e.tensor_scalar(tmp2[0:64, :], P[b][0:64, :], 32.0, -32.0, ALU.min, ALU.max), ve, 1)
            bank_set(c, b, ve)
            ve_cl = ve.n

            @c.block.scalar
            def _(e):
                e.wait_ge(ve.h, ve_cl)
                if tmp1_st is not None:
                    e.wait_ge(st3a.h, tmp1_st)
                inc(e.activation(tmp1[0:64, :], tmp2[0:64, :], AF.Exp, bias=-32.0), sc, 1)
            sc_e = sc.n
            sil_prev = sc.n

            @c.block.sync
            def _(e):
                e.wait_ge(sc.h, sc_e)
                inc(e.dma_start(out=c.dram["e3s"][0:64, h * T + t0:h * T + t0 + CH],
                                in_=tmp1[0:64, :]), st3a, 16)
            tmp1_st = st3a.n
        for gt in range(NC):
            b = 2 + (gt % 2)

            @c.block.tensor
            def _(e):
                bank_wait(c, e, b)
                m = None
                for ci in range(NC):
                    m = e.matmul(P[b][:, :], wg_sb[:, ci * 1024 + gt * 128:ci * 1024 + (gt + 1) * 128],
                                 c.xT[ci][:, t0:t0 + CH], start=(ci == 0), stop=(ci == NC - 1))
                inc(m, pe, 1)
            pe_pr = pe.n

            @c.block.vector
            def _(e):
                e.wait_ge(pe.h, pe_pr)
                if obf_st is not None:
                    e.wait_ge(st3a.h, obf_st)
                inc(e.tensor_copy(obf[:, :], P[b][:, :]), ve, 1)
            bank_set(c, b, ve)
            ve_g = ve.n

            @c.block.sync
            def _(e):
                e.wait_ge(ve.h, ve_g)
                inc(e.dma_start(out=c.dram["g3s"][gt * 128:(gt + 1) * 128, t0:t0 + CH],
                                in_=obf[:, :]), st3a, 16)
            obf_st = st3a.n
        tab3_free = (ve, ve.n)

    if os.environ.get("P3DBG"):
        st3a_now = st3a.n

        @c.block.sync
        def _(e):
            e.wait_ge(st3a.h, st3a_now)
            for nm, rows, cols in (("q3s", 512, 2048), ("k3s", 512, 2048), ("v3s", D, 2048),
                                   ("g3s", D, 2048), ("e3s", 64, 4 * T)):
                dt_ = F32 if nm == "e3s" else BF16
                dd = dout(c, "dbg3_" + nm, (rows, cols), dt_)
                inc(e.dma_start(out=dd[:, :], in_=c.dram[nm][0:rows, 0:cols]), dsem(c, "smisc"), 16)
    peA, scA, veA = pe.n, sc.n, ve.n
    st3a_all = st3a.n

    @c.block.vector
    def _(e):
        e.wait_ge(pe.h, peA)
        e.wait_ge(sc.h, scA)
        e.wait_ge(st3a.h, st3a_all)
        inc(e.memset(c.cpool[:, 6:7], 0.0), ve, 1)
    bar3b = ve.n

    for cm in reversed(a_cms):
        cm.__exit__(None, None, None)

    wo_sb = [alloc(f"p3wo{i}", [128, D], BF16) for i in range(NC)]
    triu = alloc("p3triu", [128, 128], BF16)
    qb = alloc("p3qb", [128, 2048], BF16)
    kb = alloc("p3kb", [128, 2048], BF16)
    vb = alloc("p3vb", [128, 4096], BF16)
    gb = alloc("p3gb", [128, 4096], BF16)
    ebf = alloc("p3ebf", [64, 2048], F32)
    ebb = alloc("p3ebb", [64, 2048], BF16)
    Mb = alloc("p3Mb", [64, 2048], F32)
    Mcar = alloc("p3Mcar", [64, 8], F32)
    zz = alloc("p3zz", [64, 512], F32)
    HK0s = alloc("p3HK0s", [128, 256], BF16)
    HV0s = alloc("p3HV0s", [64, 1024], BF16)
    ET_sb = alloc("p3ET", [128, 64], BF16)
    ktok = alloc("p3ktok", [128, 128], BF16)
    vtok = alloc("p3vtok", [128, 256], BF16)
    atm = alloc("p3atm", [128, 128], BF16)
    wtm = alloc("p3wtm", [128, 128], BF16)
    ok1 = alloc("p3ok1", [64, 128], BF16)
    qv2 = alloc("p3qv2", [128, 64], BF16)
    ptb = alloc("p3ptb", [64, 128], BF16)
    acol3 = alloc("p3acol", [128, 8], F32)
    rse = alloc("p3rse", [128, 128], F32)
    sg3 = alloc("p3sg", [128, 128], F32)
    t13 = alloc("p3t13", [128, 256], F32)
    obuf = alloc("p3obuf", [128, 4096], BF16)
    _qbf = qb[:, :].bitcast(F32)
    _kbf = kb[:, :].bitcast(F32)
    _vbf = vb[:, :].bitcast(F32)
    rbuf3 = [_qbf[:, 0:512], _qbf[:, 512:1024], _kbf[:, 0:512], _kbf[:, 512:1024],
             _vbf[:, 0:512], _vbf[:, 512:1024], _vbf[:, 1024:1536], _vbf[:, 1536:2048]]
    _gbf = gb[:, :].bitcast(F32)
    c.lnsq = _gbf[:, 0:1024]
    c.lnrow = _gbf[0:1, 1024:2048]

    s3ld = dsem(c, "s3ld")

    @c.block.gpsimd
    def _(e):
        e.wait_ge(ve.h, bar3b)
        for i in range(NC):
            inc(e.dma_start(out=wo_sb[i][:, :], in_=c.dram["l1wo"][i * 128:(i + 1) * 128, :]), sw3, 16)
    sw3_wo = sw3.n

    gnw = alloc("p3gnw", [128, 2], F32)

    @c.block.sync
    def _(e):
        e.wait_ge(ve.h, bar3b)
        e.wait_ge(sco.h, sco.n)
        inc(e.dma_start(out=triu[:, :], in_=c.dram["triu_b"][:, :]), sco, 16)
        inc(e.dma_start(out=gnw[:, :], in_=c.dram["gnw3"][:, :]), sco, 16)
    sco_3b2 = sco.n

    @c.block.vector
    def _(e):
        e.wait_ge(ve.h, bar3b)
        e.memset(zz[:, :], 0.0)
        inc(e.memset(Mcar[:, :], 0.0), ve, 1)
    ve_mc0 = ve.n

    ld_free = None
    first_hk = True
    for chn in range(NCH):
        t0 = chn * CH

        @c.block.sync
        def _(e):
            if chn == 0:
                e.wait_ge(ve.h, bar3b)
            if ld_free is not None:
                for s, n in ld_free:
                    e.wait_ge(s.h, n)
            inc(e.dma_start(out=qb[:, :].rearrange("p (a f) -> p a f", a=4),
                            in_=c.dram["q3s"][:, t0:t0 + CH].rearrange("(a p) f -> p a f", p=128)), s3ld, 16)
            inc(e.dma_start(out=kb[:, :].rearrange("p (a f) -> p a f", a=4),
                            in_=c.dram["k3s"][:, t0:t0 + CH].rearrange("(a p) f -> p a f", p=128)), s3ld, 16)
            inc(e.dma_start(out=vb[:, :].rearrange("p (a f) -> p a f", a=8),
                            in_=c.dram["v3s"][:, t0:t0 + CH].rearrange("(a p) f -> p a f", p=128)), s3ld, 16)
            inc(e.dma_start(out=gb[:, :].rearrange("p (a f) -> p a f", a=8),
                            in_=c.dram["g3s"][:, t0:t0 + CH].rearrange("(a p) f -> p a f", p=128)), s3ld, 16)
            inc(e.dma_start(out=ebf[:, :].rearrange("p (h f) -> p h f", h=4),
                            in_=c.dram["e3s"][:, :].rearrange("p (h t) -> p h t", h=4)[:, :, t0:t0 + CH]), s3ld, 16)
        s3_ld = s3ld.n

        @c.block.vector
        def _(e):
            e.wait_ge(s3ld.h, s3_ld)
            if chn == 0:
                e.wait_ge(ve.h, ve_mc0)
            else:
                e.wait_ge(ve.h, c.p3_carry_ve)
            inc(e.tensor_copy(ebb[:, :], ebf[:, :]), ve, 1)
            m = None
            for h in range(4):
                m = e.tensor_tensor_scan(Mb[:, h * 512:(h + 1) * 512], zz[:, :],
                                         ebf[:, h * 512:(h + 1) * 512],
                                         Mcar[:, h:h + 1], ALU.add, ALU.add)
            inc(m, ve, 1)
            e.wait_ge(ve.h, ve.n)
            m = None
            for h in range(4):
                m = e.tensor_copy(Mcar[:, h:h + 1], Mb[:, h * 512 + 511:h * 512 + 512])
            inc(m, ve, 1)
            e.wait_ge(ve.h, ve.n)
            inc(e.reciprocal(Mb[:, :], Mb[:, :]), ve, 1)
        c.p3_carry_ve = ve.n
        ve_mass = ve.n

        for cc in range(4):
            tc = slice(cc * 128, (cc + 1) * 128)
            @c.block.vector
            def _(e):
                if not first_hk:
                    e.wait_ge(pe.h, c.p3_upd_pe)
                    e.tensor_copy(HK0s[:, :], P[0][:, 0:256])
                    e.tensor_copy(HV0s[:, 0:512], P[1][0:64, 0:512])
                    inc(e.tensor_copy(HV0s[:, 512:1024], P[2][0:64, 0:512]), ve, 1)
                else:
                    e.memset(HK0s[:, :], 0.0)
                    inc(e.memset(HV0s[:, :], 0.0), ve, 1)
            ve_snap = ve.n

            for h in range(4):
                qf = qb[:, h * 512 + cc * 128:h * 512 + (cc + 1) * 128]
                kf = kb[:, h * 512 + cc * 128:h * 512 + (cc + 1) * 128]
                eslice = ebb[:, h * 512 + cc * 128:h * 512 + (cc + 1) * 128]
                Mslice = Mb[:, h * 512 + cc * 128:h * 512 + (cc + 1) * 128]

                @c.block.tensor
                def _(e):
                    bank_wait(c, e, 3)
                    if chn == 1 and h == 0 and cc == 0 and getattr(c, "p3dump_gate", None):
                        e.wait_ge(dsem(c, "smisc").h, c.p3dump_gate)
                    if chn == 0 and cc == 0 and h == 1 and getattr(c, "p3gate2", None):
                        e.wait_ge(dsem(c, "smisc").h, c.p3gate2)
                    if h == 0 and cc == 0:
                        e.wait_ge(ve.h, ve_mass)
                    if first_hk and h == 0 and cc == 0:
                        e.wait_ge(sw3.h, sw3_wo)
                        e.wait_ge(sco.h, sco_3b2)
                    P3b = P[3][:, :].bitcast(BF16)
                    e.transpose(P3b[:, 0:64], eslice, c.idb[0:64, 0:64])
                    e.transpose(P3b[:, 64:192], kf, c.idb[:, :])
                    e.transpose(P3b[:, 192:320], vb[:, (2 * h) * 512 + cc * 128:(2 * h) * 512 + (cc + 1) * 128], c.idb[:, :])
                    inc(e.transpose(P3b[:, 320:448], vb[:, (2 * h + 1) * 512 + cc * 128:(2 * h + 1) * 512 + (cc + 1) * 128], c.idb[:, :]), pe, 1)
                pe_tp1 = pe.n

                @c.block.vector
                def _(e):
                    e.wait_ge(pe.h, pe_tp1)
                    if h == 0:
                        e.wait_ge(ve.h, ve_snap)
                    P3b = P[3][:, :].bitcast(BF16)
                    e.tensor_copy(ET_sb[:, :], P3b[:, 0:64])
                    e.tensor_copy(ktok[:, :], P3b[:, 64:192])
                    inc(e.tensor_copy(vtok[:, :], P3b[:, 192:448]), ve, 1)
                bank_set(c, 3, ve)
                ve_tp1 = ve.n

                @c.block.tensor
                def _(e):
                    bank_wait(c, e, 4)
                    inc(e.matmul(P[4][:, 0:128], kf, qf, start=True, stop=True), pe, 1)
                pe_at = pe.n

                @c.block.vector
                def _(e):
                    e.wait_ge(pe.h, pe_at)
                    inc(e.tensor_mul(atm[:, :], P[4][:, 0:128], triu[:, :]), ve, 1)
                bank_set(c, 4, ve)
                ve_at = ve.n

                @c.block.tensor
                def _(e):
                    bank_wait(c, e, 5)
                    e.wait_ge(ve.h, max(ve_at, ve_tp1))
                    e.matmul(P[5][0:64, 0:128], ET_sb[:, :], atm[:, :], start=True, stop=False)
                    inc(e.matmul(P[5][0:64, 0:128], HK0s[:, h * 64:(h + 1) * 64], qf,
                                 start=False, stop=True), pe, 1)
                pe_ib = pe.n

                @c.block.vector
                def _(e):
                    e.wait_ge(pe.h, pe_ib)
                    inc(e.tensor_mul(ok1[:, :], P[5][0:64, 0:128], Mslice), ve, 1)
                bank_set(c, 5, ve)
                ve_ok = ve.n

                @c.block.tensor
                def _(e):
                    bank_wait(c, e, 7)
                    e.wait_ge(ve.h, ve_ok)
                    P7b = P[7][:, :].bitcast(BF16)
                    inc(e.transpose(P7b[:, 0:64], ok1[:, :], c.idb[0:64, 0:64]), pe, 1)
                pe_tok = pe.n

                @c.block.vector
                def _(e):
                    e.wait_ge(pe.h, pe_tok)
                    inc(e.reduce_max(acol3[:, 0:1], P[7][:, :].bitcast(BF16)[:, 0:64],
                                     axis=AX, negate=True), ve, 1)
                ve_rm = ve.n

                @c.block.scalar
                def _(e):
                    e.wait_ge(ve.h, ve_rm)
                    inc(e.activation(qv2[:, :], P[7][:, :].bitcast(BF16)[:, 0:64], AF.Exp,
                                     bias=acol3[:, 0:1], accum_out=acol3[:, 1:2]), sc, 1)
                sc_qv = sc.n

                @c.block.vector
                def _(e):
                    e.wait_ge(sc.h, sc_qv)
                    inc(e.reciprocal(acol3[:, 2:3], acol3[:, 1:2]), ve, 1)
                ve_rd = ve.n

                @c.block.scalar
                def _(e):
                    e.wait_ge(ve.h, ve_rd)
                    inc(e.mul(qv2[:, :], qv2[:, :], acol3[:, 2:3]), sc, 1)
                sc_qs = sc.n

                @c.block.tensor
                def _(e):
                    e.wait_ge(sc.h, sc_qs)
                    P7b = P[7][:, :].bitcast(BF16)
                    inc(e.transpose(P7b[0:64, 64:192], qv2[:, :], c.idb[:, :]), pe, 1)
                pe_tqv = pe.n

                @c.block.vector
                def _(e):
                    e.wait_ge(pe.h, pe_tqv)
                    inc(e.tensor_mul(ptb[:, :], P[7][:, :].bitcast(BF16)[0:64, 64:192],
                                     Mslice), ve, 1)
                bank_set(c, 7, ve)
                ve_pt = ve.n

                @c.block.tensor
                def _(e):
                    bank_wait(c, e, 4)
                    e.wait_ge(ve.h, ve_pt)
                    inc(e.matmul(P[4][:, 0:128], eslice, ptb[:, :], start=True, stop=True), pe, 1)
                pe_wt = pe.n

                @c.block.vector
                def _(e):
                    e.wait_ge(pe.h, pe_wt)
                    inc(e.tensor_mul(wtm[:, :], P[4][:, 0:128], triu[:, :]), ve, 1)
                bank_set(c, 4, ve)
                ve_wt = ve.n

                @c.block.tensor
                def _(e):
                    bank_wait(c, e, 6)
                    e.wait_ge(ve.h, ve_wt)
                    e.matmul(P[6][:, 0:128], vtok[:, 0:128], wtm[:, :], start=True, stop=False)
                    e.matmul(P[6][:, 0:128], HV0s[0:64, h * 256:h * 256 + 128], ptb[:, :],
                             start=False, stop=True)
                    e.matmul(P[6][:, 128:256], vtok[:, 128:256], wtm[:, :], start=True, stop=False)
                    inc(e.matmul(P[6][:, 128:256], HV0s[0:64, h * 256 + 128:h * 256 + 256],
                                 ptb[:, :], start=False, stop=True), pe, 1)
                pe_ov = pe.n

                first_here = (chn == 0 and cc == 0)

                @c.block.tensor
                def _(e):
                    e.matmul(P[0][:, h * 64:(h + 1) * 64], ktok[:, :], ET_sb[:, :],
                             start=first_here, stop=False, skip_group_check=True)
                    bankv = P[1] if h < 2 else P[2]
                    colv = (h % 2) * 256
                    inc(e.matmul(bankv[0:64, colv:colv + 256], ET_sb[:, :], vtok[:, :],
                                 start=first_here, stop=False, skip_group_check=True), pe, 1)
                c.p3_upd_pe = pe.n

                @c.block.scalar
                def _(e):
                    e.wait_ge(pe.h, pe_ov)
                    e.square(t13[:, 0:128], P[6][:, 0:128])
                    inc(e.square(t13[:, 128:256], P[6][:, 128:256]), sc, 1)
                sc_sq = sc.n

                @c.block.tensor
                def _(e):
                    bank_wait(c, e, 4)
                    e.wait_ge(sc.h, sc_sq)
                    e.matmul(P[4][0:1, 0:128], c.onescol_f[:, :], t13[:, 0:128], start=True, stop=False)
                    inc(e.matmul(P[4][0:1, 0:128], c.onescol_f[:, :], t13[:, 128:256],
                                 start=False, stop=True), pe, 1)
                pe_var = pe.n

                @c.block.vector
                def _(e):
                    e.wait_ge(pe.h, pe_var)
                    inc(e.tensor_scalar_mul(rse[0:1, 0:128], P[4][0:1, 0:128], 1.0 / 256.0), ve, 1)
                bank_set(c, 4, ve)
                ve_var3 = ve.n

                @c.block.scalar
                def _(e):
                    e.wait_ge(ve.h, ve_var3)
                    inc(e.activation(sg3[0:1, 0:128], rse[0:1, 0:128], AF.Sqrt, bias=1e-5), sc, 1)
                sc_sd3 = sc.n

                @c.block.vector
                def _(e):
                    e.wait_ge(sc.h, sc_sd3)
                    inc(e.reciprocal(sg3[0:1, 0:128], sg3[0:1, 0:128]), ve, 1)
                ve_rs3 = ve.n

                @c.block.tensor
                def _(e):
                    bank_wait(c, e, 5)
                    e.wait_ge(ve.h, ve_rs3)
                    inc(e.matmul(P[5][:, 0:128], c.onesall[0:1, :], sg3[0:1, 0:128],
                                 start=True, stop=True), pe, 1)
                pe_rse = pe.n

                @c.block.vector
                def _(e):
                    e.wait_ge(pe.h, pe_rse)
                    inc(e.tensor_copy(rse[:, :], P[5][:, 0:128]), ve, 1)
                bank_set(c, 5, ve)
                ve_rse = ve.n

                for vt in range(2):
                    @c.block.scalar
                    def _(e):
                        if vt == 0:
                            e.wait_ge(ve.h, ve_rse)
                        inc(e.activation(sg3[:, 0:128], gb[:, (2 * h + vt) * 512 + cc * 128:(2 * h + vt) * 512 + (cc + 1) * 128], AF.Silu), sc, 1)
                    sc_g3 = sc.n

                    @c.block.vector
                    def _(e):
                        e.wait_ge(sc.h, sc_g3)
                        inc(e.scalar_tensor_tensor(t13[:, 0:128], P[6][:, vt * 128:(vt + 1) * 128],
                                                   gnw[:, vt:vt + 1], rse[:, :], ALU.mult, ALU.mult), ve, 1)
                        e.wait_ge(ve.h, ve.n)
                        inc(e.tensor_mul(obuf[:, (2 * h + vt) * 512 + cc * 128:(2 * h + vt) * 512 + (cc + 1) * 128], t13[:, 0:128],
                                         sg3[:, 0:128]), ve, 1)
                    if vt == 1:
                        bank_set(c, 6, ve)
                if os.environ.get("P3DBG2") and chn == 0 and cc == 0 and h == 0:
                    dov = dout(c, "dbgs_ov", (128, 256), F32)
                    dhv = dout(c, "dbgs_HV0", (64, 1024), BF16)
                    ovd = alloc("p3dbgov", [128, 256], F32)
                    pe_ov_d = pe_ov

                    @c.block.vector
                    def _(e):
                        e.wait_ge(pe.h, pe_ov_d)
                        inc(e.tensor_copy(ovd[:, :], P[6][:, 0:256]), ve, 1)
                    ve_ovd = ve.n

                    @c.block.sync
                    def _(e):
                        e.wait_ge(ve.h, ve_ovd)
                        inc(e.dma_start(out=dov[:, :], in_=ovd[:, :]), dsem(c, "smisc"), 16)
                        inc(e.dma_start(out=dhv[:, :], in_=HV0s[:, :]), dsem(c, "smisc"), 16)
                    dd_ = {}
                    for nm, tns, shp in (("atm", atm, (128, 128)), ("ok1", ok1, (64, 128)),
                                         ("qv2", qv2, (128, 64)), ("ptb", ptb, (64, 128)),
                                         ("wtm", wtm, (128, 128)), ("ETs", ET_sb, (128, 64)),
                                         ("ktok", ktok, (128, 128)), ("vtok", vtok, (128, 256)),
                                         ("obub", obuf, (128, 4096))):
                        dd_[nm] = dout(c, "dbgs_" + nm, shp, BF16)
                    dd_["Mbd"] = dout(c, "dbgs_Mbd", (64, 2048), F32)
                    ve_dmp = ve.n

                    @c.block.sync
                    def _(e):
                        e.wait_ge(ve.h, ve_dmp)
                        for nm, tns in (("atm", atm), ("ok1", ok1), ("qv2", qv2), ("ptb", ptb),
                                        ("wtm", wtm), ("ETs", ET_sb), ("ktok", ktok),
                                        ("vtok", vtok), ("obub", obuf)):
                            inc(e.dma_start(out=dd_[nm][:, :], in_=tns[:, :] if nm != "obub" else tns[:, :]),
                                dsem(c, "smisc"), 16)
                        inc(e.dma_start(out=dd_["Mbd"][:, :], in_=Mb[:, :]), dsem(c, "smisc"), 16)
                    c.p3gate2 = dsem(c, "smisc").n
                first_hk = False

        ve_obuf = ve.n

        if os.environ.get("P3DBG") and chn == 0:
            dob = dout(c, "dbg3_obuf", (128, 4096), BF16)

            @c.block.sync
            def _(e):
                e.wait_ge(ve.h, ve_obuf)
                inc(e.dma_start(out=dob[:, :], in_=obuf[:, :]), dsem(c, "smisc"), 16)
            c.p3dump_gate = dsem(c, "smisc").n

        for ot in range(NC):
            b = 3 if ot % 2 == 0 else 7

            @c.block.tensor
            def _(e):
                bank_wait(c, e, b)
                if ot == 0:
                    e.wait_ge(ve.h, ve_obuf)
                m = None
                for ci in range(NC):
                    m = e.matmul(P[b][:, :], wo_sb[ci][:, ot * 128:(ot + 1) * 128],
                                 obuf[:, ci * 512:(ci + 1) * 512], start=(ci == 0), stop=(ci == NC - 1))
                inc(m, pe, 1)
            pe_wo3 = pe.n

            @c.block.vector
            def _(e):
                e.wait_ge(pe.h, pe_wo3)
                inc(e.tensor_add(rbuf3[ot], P[b][:, :], c.xT[ot][:, t0:t0 + CH]), ve, 1)
            bank_set(c, b, ve)
        ve_r3 = ve.n

        def wr3(ot, tmp, e, t0=t0):
            inc(e.tensor_mul(c.xT[ot][:, t0:t0 + CH], tmp, c.P[4][:, 0:CH]), ve, 1)

        emit_ln2(c, rbuf3, CH, wr3, [(ve, ve_r3)] * NC)
        ld_free = [(pe, pe.n), (ve, ve.n)]

    for cm in reversed(phase_cms):
        cm.__exit__(None, None, None)



def phase4_inputs(c):
    for e_ in range(4):
        din(c, f"moeW1_{e_}", (D, 4096), F32)
        din(c, f"moeW2_{e_}", (4096, D), F32)
        din(c, f"moeb1_{e_}", (128, 32), F32)
        din(c, f"moeb2_{e_}", (128, NC), F32)
        dscratch(c, f"w1mb{e_}", (D, 4096), BF16)
        dscratch(c, f"w2mb{e_}", (4096, D), BF16)
    din(c, "moewg", (D, 4), F32)
    din(c, "moebgE", (128, 4), F32)
    dscratch(c, "ohs", (4, T), F32)
    dout(c, "out", (T, D), F32)


def phase4_host(h, inp):
    for e_ in range(4):
        h[f"moeW1_{e_}"] = np.ascontiguousarray(inp["L1_ffn_W1"][e_])
        h[f"moeW2_{e_}"] = np.ascontiguousarray(inp["L1_ffn_W2"][e_])
        h[f"moeb1_{e_}"] = np.ascontiguousarray(inp["L1_ffn_b1"][e_].reshape(32, 128).T)
        h[f"moeb2_{e_}"] = np.ascontiguousarray(inp["L1_ffn_b2"][e_].reshape(NC, 128).T)
    h["moewg"] = np.ascontiguousarray(inp["L1_ffn_Wg"])
    h["moebgE"] = np.tile(inp["L1_ffn_bg"].reshape(1, 4), (128, 1)).copy()


def emit_phase4(c):
    nc, P = c.nc, c.P
    pe, ve, sc = c.pe, c.ve, c.sc
    phase_cms = []

    def alloc(name, shape, dtype):
        cm = nc.sbuf_tensor(name, shape, dtype)
        phase_cms.append(cm)
        return cm.__enter__()

    peB, scB = pe.n, sc.n

    @c.block.vector
    def _(e):
        e.wait_ge(pe.h, peB)
        e.wait_ge(sc.h, scB)
        inc(e.memset(c.cpool[:, 6:7], 0.0), ve, 1)
    bar4 = ve.n

    sco = dsem(c, "sconst")
    sc4 = dsem(c, "sconv4")
    sg4 = dsem(c, "sg4")
    sw1m = [dsem(c, "sw1m0"), dsem(c, "sw1m1")]
    sw2m = [dsem(c, "sw2m0"), dsem(c, "sw2m1")]
    soh = dsem(c, "soh")
    sout = dsem(c, "sout")

    @c.block.gpsimd
    def _(e):
        e.wait_ge(ve.h, bar4)
        for e_ in range(4):
            for j in range(2):
                inc(e.dma_start(out=c.dram[f"w1mb{e_}"][j * 512:(j + 1) * 512, :],
                                in_=c.dram[f"moeW1_{e_}"][j * 512:(j + 1) * 512, :]), sc4, 16)
                inc(e.dma_start(out=c.dram[f"w2mb{e_}"][j * 2048:(j + 1) * 2048, :],
                                in_=c.dram[f"moeW2_{e_}"][j * 2048:(j + 1) * 2048, :]), sc4, 16)
    sc4_n = sc4.n

    wgsb = alloc("p4wg", [128, 32], BF16)
    bgE = alloc("p4bg", [128, 4], F32)
    b1c = [alloc(f"p4b1_{e_}", [128, 32], F32) for e_ in range(4)]
    b2c = [alloc(f"p4b2_{e_}", [128, NC], F32) for e_ in range(4)]
    ohtm = alloc("p4ohtm", [128, 4], F32)
    ohcs = alloc("p4ohcs", [128, 4], F32)
    ohz = alloc("p4ohz", [128, 4], F32)
    ohbf = alloc("p4ohbf", [128, 4], BF16)
    ohst = alloc("p4ohst", [4, 128], F32)
    mcol = alloc("p4mcol", [128, 1], F32)

    @c.block.gpsimd
    def _(e):
        e.wait_ge(ve.h, bar4)
        inc(e.dma_start(out=wgsb[:, :].rearrange("p (a f) -> p a f", a=8),
                        in_=c.dram["moewg"][:, :].rearrange("(a p) f -> p a f", p=128)), sg4, 16)
    sg4_wg = sg4.n

    @c.block.sync
    def _(e):
        e.wait_ge(ve.h, bar4)
        e.wait_ge(sco.h, sco.n)
        inc(e.dma_start(out=bgE[:, :], in_=c.dram["moebgE"][:, :]), sco, 16)
        for e_ in range(4):
            inc(e.dma_start(out=b1c[e_][:, :], in_=c.dram[f"moeb1_{e_}"][:, :]), sco, 16)
            inc(e.dma_start(out=b2c[e_][:, :], in_=c.dram[f"moeb2_{e_}"][:, :]), sco, 16)
    sco_4 = sco.n

    @c.block.vector
    def _(e):
        inc(e.memset(ohz[:, :], 0.0), ve, 1)
    ve_z4 = ve.n

    for tt in range(NT):
        @c.block.tensor
        def _(e):
            bank_wait(c, e, 4)
            if tt == 0:
                e.wait_ge(sco.h, sco_4)
                e.wait_ge(sg4.h, sg4_wg)
                e.wait_ge(ve.h, ve_z4)
            m = None
            for ci in range(NC):
                m = e.matmul(P[4][:, 0:4], c.xT[ci][:, tt * 128:(tt + 1) * 128],
                             wgsb[:, ci * 4:(ci + 1) * 4], start=(ci == 0), stop=(ci == NC - 1))
            inc(m, pe, 1)
        pe_g4 = pe.n

        @c.block.vector
        def _(e):
            e.wait_ge(pe.h, pe_g4)
            inc(e.tensor_add(ohtm[:, :], P[4][:, 0:4], bgE[:, :]), ve, 1)
            e.wait_ge(ve.h, ve.n)
            inc(e.reduce_max(mcol[:, :], ohtm[:, :], axis=AX), ve, 1)
            e.wait_ge(ve.h, ve.n)
            inc(e.tensor_scalar(ohtm[:, :], ohtm[:, :], mcol[:, 0:1], None, ALU.is_equal), ve, 1)
            e.wait_ge(ve.h, ve.n)
            inc(e.tensor_tensor_scan(ohcs[:, :], ohz[:, :], ohtm[:, :], 0.0, ALU.add, ALU.add), ve, 1)
            e.wait_ge(ve.h, ve.n)
            inc(e.tensor_single_scalar(ohcs[:, :], ohcs[:, :], 1.0, ALU.is_equal), ve, 1)
            e.wait_ge(ve.h, ve.n)
            inc(e.tensor_mul(ohbf[:, :], ohtm[:, :], ohcs[:, :]), ve, 1)
        bank_set(c, 4, ve)
        ve_oh = ve.n

        @c.block.tensor
        def _(e):
            bank_wait(c, e, 5)
            e.wait_ge(ve.h, ve_oh)
            inc(e.transpose(P[5][:, :].bitcast(BF16)[0:4, 0:128], ohbf[:, :], c.idb[:, :]), pe, 1)
        pe_oht = pe.n

        @c.block.vector
        def _(e):
            e.wait_ge(pe.h, pe_oht)
            inc(e.tensor_copy(ohst[0:4, :], P[5][:, :].bitcast(BF16)[0:4, 0:128]), ve, 1)
        bank_set(c, 5, ve)
        ve_ohst = ve.n

        @c.block.sync
        def _(e):
            e.wait_ge(ve.h, ve_ohst)
            e.wait_ge(soh.h, soh.n)
            inc(e.dma_start(out=c.dram["ohs"][0:4, tt * 128:(tt + 1) * 128],
                            in_=ohst[0:4, :]), soh, 16)
    soh_all = soh.n

    hbuf = alloc("p4h", [128, 32 * 512], BF16)
    w1st = alloc("p4w1st", [128, 2 * 4096], BF16)
    w2st = alloc("p4w2st", [128, 2 * 4096], BF16)
    ohE4 = alloc("p4ohE4", [128, 512], BF16)
    ohrow = alloc("p4ohrow", [1, 512], F32)
    yacc = [alloc(f"p4y{i}", [128, 512], BF16) for i in range(NC)]
    ytmp = alloc("p4ytmp", [128, 512], BF16)
    _hf = hbuf[:, :].bitcast(F32)
    rbuf4 = [_hf[:, i * 512:(i + 1) * 512] for i in range(NC)]
    c.lnsq = _hf[:, 4096:5120]
    c.lnrow = _hf[0:1, 5120:6144]
    oo = rbuf4
    outst = _hf[:, 6144:7168]

    w1_free = [None, None]
    w2_free = [None, None]
    ohE_free = None
    h_free = None
    out_free = None

    for chn in range(NCH):
        t0 = chn * CH

        for e_ in range(4):
            @c.block.sync
            def _(e):
                if chn == 0 and e_ == 0:
                    e.wait_ge(soh.h, soh_all)
                if ohE_free is not None:
                    s, n = ohE_free
                    e.wait_ge(s.h, n)
                inc(e.dma_start(out=ohrow[0:1, :],
                                in_=c.dram["ohs"][e_:e_ + 1, t0:t0 + CH]), sg4, 16)
            sg4_ld = sg4.n
            ohE_free = None

            @c.block.tensor
            def _(e):
                bank_wait(c, e, 7)
                e.wait_ge(sg4.h, sg4_ld)
                inc(e.matmul(P[7][:, 0:512], c.onesall[0:1, :],
                             ohrow[0:1, :], start=True, stop=True), pe, 1)
            pe_ohE = pe.n

            @c.block.vector
            def _(e):
                e.wait_ge(pe.h, pe_ohE)
                inc(e.tensor_copy(ohE4[:, :], P[7][:, 0:512]), ve, 1)
            bank_set(c, 7, ve)
            ve_ohE = ve.n
            for kg in range(8):
                slot = kg % 2

                @c.block.sync
                def _(e):
                    if chn == 0 and e_ == 0 and kg == 0:
                        e.wait_ge(sc4.h, sc4_n)
                    if w1_free[slot] is not None:
                        s, n = w1_free[slot]
                        e.wait_ge(s.h, n)
                    inc(e.dma_start(out=w1st[:, slot * 4096:(slot + 1) * 4096]
                                    .rearrange("p (a f) -> p a f", a=8),
                                    in_=c.dram[f"w1mb{e_}"][:, kg * 512:(kg + 1) * 512]
                                    .rearrange("(a p) f -> p a f", p=128)), sw1m[slot], 16)
                sl_w1 = sw1m[slot].n

                for kk in range(4):
                    kidx = kg * 4 + kk
                    b = kidx % 2

                    @c.block.tensor
                    def _(e):
                        bank_wait(c, e, b)
                        e.wait_ge(sw1m[slot].h, sl_w1)
                        if chn == 0 and e_ == 0 and kidx == 0:
                            pass
                        if h_free is not None and kidx == 0:
                            s, n = h_free
                            e.wait_ge(s.h, n)
                        m = None
                        for ci in range(NC):
                            m = e.matmul(P[b][:, :],
                                         w1st[:, slot * 4096 + ci * 512 + kk * 128:
                                              slot * 4096 + ci * 512 + (kk + 1) * 128],
                                         c.xT[ci][:, t0:t0 + CH], start=(ci == 0), stop=(ci == NC - 1))
                        inc(m, pe, 1)
                    pe_w1m = pe.n

                    @c.block.scalar
                    def _(e):
                        e.wait_ge(pe.h, pe_w1m)
                        inc(e.activation(hbuf[:, kidx * 512:(kidx + 1) * 512], P[b][:, :],
                                         AF.Gelu, bias=b1c[e_][:, kidx:kidx + 1]), sc, 1)
                    bank_set(c, b, sc)
                w1_free[slot] = (pe, pe.n)
            sc_h = sc.n

            for kg in range(8):
                slot = kg % 2

                @c.block.scalar
                def _(e):
                    if chn == 0 and e_ == 0 and kg == 0:
                        e.wait_ge(sc4.h, sc4_n)
                    if w2_free[slot] is not None:
                        s, n = w2_free[slot]
                        e.wait_ge(s.h, n)
                    inc(e.dma_start(out=w2st[:, slot * 4096:(slot + 1) * 4096]
                                    .rearrange("p (a f) -> p a f", a=4),
                                    in_=c.dram[f"w2mb{e_}"][kg * 512:(kg + 1) * 512, :]
                                    .rearrange("(a p) f -> p a f", p=128)), sw2m[slot], 16)
                sl_w2 = sw2m[slot].n

                @c.block.tensor
                def _(e):
                    e.wait_ge(sw2m[slot].h, sl_w2)
                    if kg == 0:
                        e.wait_ge(sc.h, sc_h)
                        for bb in range(8):
                            bank_wait(c, e, bb)
                    m = None
                    for kk in range(4):
                        kidx = kg * 4 + kk
                        for ot in range(NC):
                            m = e.matmul(P[ot][:, :],
                                         w2st[:, slot * 4096 + kk * 1024 + ot * 128:
                                              slot * 4096 + kk * 1024 + (ot + 1) * 128],
                                         hbuf[:, kidx * 512:(kidx + 1) * 512],
                                         start=(kidx == 0), stop=(kidx == 31))
                    inc(m, pe, 1)
                w2_free[slot] = (pe, pe.n)
            pe_w2m = pe.n
            h_free = (pe, pe.n)

            if os.environ.get("P4DBG") and chn == 0 and e_ == 0:
                dohe = dout(c, "dbg4_ohE0", (128, 512), BF16)
                sgnow = sg4.n

                @c.block.sync
                def _(e):
                    e.wait_ge(sg4.h, sgnow)
                    inc(e.dma_start(out=dohe[:, :], in_=ohE4[:, :]), dsem(c, "smisc"), 16)

            for ot in range(NC):
                @c.block.scalar
                def _(e):
                    e.wait_ge(pe.h, pe_w2m)
                    inc(e.add(P[ot][:, :], P[ot][:, :], b2c[e_][:, ot:ot + 1]), sc, 1)
                sc_b2 = sc.n

                @c.block.vector
                def _(e):
                    e.wait_ge(sc.h, sc_b2)
                    if ot == 0:
                        e.wait_ge(ve.h, ve_ohE)
                    if e_ == 0:
                        inc(e.tensor_mul(yacc[ot][:, :], P[ot][:, :], ohE4[:, :]), ve, 1)

                    else:
                        inc(e.tensor_mul(ytmp[:, :], P[ot][:, :], ohE4[:, :]), ve, 1)
                        e.wait_ge(ve.h, ve.n)
                        inc(e.tensor_add(yacc[ot][:, :], yacc[ot][:, :], ytmp[:, :]), ve, 1)
                bank_set(c, ot, ve)
            ohE_free = (ve, ve.n)

        if os.environ.get("P4DBG") and chn == 0:
            dy0 = dout(c, "dbg4_y0", (128, 8 * 512), BF16)
            dy = dout(c, "dbg4_yacc", (128, 8 * 512), BF16)
            dh = dout(c, "dbg4_h", (128, 32 * 512), BF16)
            dohs = dout(c, "dbg4_ohs", (4, T), F32)
            ve_now4 = ve.n

            @c.block.sync
            def _(e):
                e.wait_ge(ve.h, ve_now4)
                for i in range(NC):
                    inc(e.dma_start(out=dy[:, i * 512:(i + 1) * 512], in_=yacc[i][:, :]),
                        dsem(c, "smisc"), 16)
                inc(e.dma_start(out=dh[:, :], in_=hbuf[:, :]), dsem(c, "smisc"), 16)
                inc(e.dma_start(out=dohs[:, :], in_=c.dram["ohs"][:, :]), dsem(c, "smisc"), 16)

        ve_ready = []
        for ot in range(NC):
            @c.block.vector
            def _(e):
                if h_free is None and ot == 0:
                    pass
                inc(e.tensor_add(rbuf4[ot], yacc[ot][:, :], c.xT[ot][:, t0:t0 + CH]), ve, 1)
            ve_ready.append((ve, ve.n))
        h_free = None

        def wr4(ot, tmp, e):
            inc(e.tensor_mul(oo[ot], tmp, c.P[4][:, 0:CH]), ve, 1)

        emit_ln2(c, rbuf4, CH, wr4, ve_ready)
        ve_oo = ve.n

        for j in range(4):
            @c.block.tensor
            def _(e):
                bank_wait(c, e, 5)
                bank_wait(c, e, 6)
                if j == 0:
                    e.wait_ge(ve.h, ve_oo)
                m = None
                for ot in range(NC):
                    bnk = P[5] if ot < 4 else P[6]
                    m = e.transpose(bnk[:, (ot % 4) * 128:(ot % 4 + 1) * 128],
                                    oo[ot][:, j * 128:(j + 1) * 128], c.idf[:, :])
                inc(m, pe, 1)
            pe_tr4 = pe.n

            @c.block.vector
            def _(e):
                e.wait_ge(pe.h, pe_tr4)
                if out_free is not None:
                    s, n = out_free
                    e.wait_ge(s.h, n)
                e.tensor_copy(outst[:, 0:512], P[5][:, :])
                inc(e.tensor_copy(outst[:, 512:1024], P[6][:, :]), ve, 1)
            bank_set(c, 5, ve)
            bank_set(c, 6, ve)
            ve_st4 = ve.n

            @c.block.sync
            def _(e):
                e.wait_ge(ve.h, ve_st4)
                inc(e.dma_start(out=c.dram["out"][t0 + j * 128:t0 + (j + 1) * 128, :],
                                in_=outst[:, :]), sout, 16)
            out_free = (sout, sout.n)
        h_free = (ve, ve.n)

    sout_all = sout.n

    @c.block.sync
    def _(e):
        e.wait_ge(sout.h, sout_all)

    for cm in reversed(phase_cms):
        cm.__exit__(None, None, None)



def emit_finalize_from_xT(c):
    out = dout(c, "out", (T, D), F32)
    nc, pe, ve, sl = c.nc, c.pe, c.ve, c.sl
    ob = _e(nc.sbuf_tensor("fin_ob", [128, D], F32))
    ve_all = ve.n
    sl_tile = []

    for tt in range(NT):
        par = 0
        for half in range(2):
            bi = 2 * par + half

            @c.block.tensor
            def _(e):
                bank_wait(c, e, bi)
                if tt == 0 and half == 0:
                    e.wait_ge(ve.h, ve_all)
                m = None
                for j in range(4):
                    ci = half * 4 + j
                    m = e.matmul(c.P[bi][:, j * 128:(j + 1) * 128],
                                 c.xT[ci][:, tt * 128:(tt + 1) * 128],
                                 c.idb[:, :], start=True, stop=True)
                inc(m, pe, 1)
            pe_t = pe.n

            @c.block.vector
            def _(e):
                e.wait_ge(pe.h, pe_t)
                if tt >= 1 and half == 0:
                    e.wait_ge(dsem(c, "sout").h, sl_tile[tt - 1])
                inc(e.tensor_copy(ob[:, par * D + half * 512: par * D + (half + 1) * 512],
                                  c.P[bi][:, :]), ve, 1)
            bank_set(c, bi, ve)
        ve_t = ve.n

        @c.block.sync
        def _(e):
            e.wait_ge(ve.h, ve_t)
            inc(e.dma_start(out=out[tt * 128:(tt + 1) * 128, :],
                            in_=ob[:, par * D:par * D + D]), dsem(c, "sout"), 16)
        sl_tile.append(dsem(c, "sout").n)
    sl_done = dsem(c, "sout").n

    @c.block.sync
    def _(e):
        e.wait_ge(dsem(c, "sout").h, sl_done)



def build(phases, save_x1=False, p2stop=None, dbgw=None):
    nc = bass.Bass("TRN2", target_bir_lowering=False)
    c = make_ctx(nc)
    const_inputs(c)
    if "p1" in phases:
        phase1_inputs(c)
    if "p2" in phases:
        phase2_inputs(c)
    if "p3" in phases:
        phase3_inputs(c)
    if "p4" in phases:
        phase4_inputs(c)
    setup_const_sbuf(c)
    if "p1" in phases:
        emit_phase1(c, save_x1=("p2" in phases) or save_x1)
    if "p2" in phases:
        emit_phase2(c, p2stop=p2stop, dbgw=dbgw)
        if p2stop:
            emit_finalize_from_xT(c)
            return nc, c
    if "p3" in phases:
        emit_phase3(c)
    if "p4" in phases:
        emit_phase4(c)
    else:
        emit_finalize_from_xT(c)
        return nc, c
    return nc, c
    return nc, c


def host_inputs(inp, phases, b):
    h = dict(host_consts())
    if "p1" in phases:
        phase1_host(h, inp, b)
    if "p2" in phases:
        phase2_host(h, inp)
    if "p3" in phases:
        phase3_host(h, inp)
    if "p4" in phases:
        phase4_host(h, inp)
    return h


import os
def kernel(**inputs):
    phases = tuple(os.environ.get("PHASES", "p1,p2,p3,p4").split(","))
    dbgw = os.environ.get("P2DBGW")
    nc, c = build(phases, p2stop=os.environ.get("P2STOP") or None,
                  dbgw=int(dbgw) if dbgw else None)
    in_maps = [host_inputs(inputs, phases, b) for b in range(8)]
    res = run_bass_kernel_spmd(nc, in_maps, core_ids=list(range(8)))
    return np.stack([res.results[b]["out"] for b in range(8)], axis=0)
